# version 14 (final)
# baseline (speedup 1.0000x reference)
import contextlib
import os
import numpy as np
NOLN = bool(os.environ.get('NOLN'))
import ml_dtypes
import concourse.bass as bass
import concourse.mybir as mybir
from concourse.alu_op_type import AluOpType as OP
from concourse.bass_utils import run_bass_kernel_spmd

F32 = mybir.dt.float32
BF16 = mybir.dt.bfloat16
I32 = mybir.dt.int32
AF = mybir.ActivationFunctionType
AX = mybir.AxisListType

B, T, D = 2, 4096, 4096
H, KV, HD = 32, 8, 128
NCORE = 8
TPC = 1024
E, R = 8, 16
NH = 32
CAPACITY = float(int(3.0 * (B * T) / E))
LN_EPS = 1e-5
RAMP_C = 60.0
NSLOT = 8

GROUPS_HALF = [[0, 1, 2, 3], [4, 5, 6, 7]]
GROUPS_ALL = [[0, 1, 2, 3, 4, 5, 6, 7]]



def _llama3_inv_freq():
    THETA, FACTOR, HI_FF, LO_FF, ORIG_MAX = 500000.0, 8.0, 4.0, 1.0, 8192
    inv = 1.0 / THETA ** (np.arange(0, HD, 2, dtype=np.float64) / HD)
    wavelen = 2.0 * np.pi / inv
    low_wl = ORIG_MAX / LO_FF
    high_wl = ORIG_MAX / HI_FF
    smooth = (ORIG_MAX / wavelen - LO_FF) / (HI_FF - LO_FF)
    scaled = np.where(wavelen > low_wl, inv / FACTOR, inv)
    mid = (wavelen <= low_wl) & (wavelen >= high_wl)
    scaled = np.where(mid, (1 - smooth) * inv / FACTOR + smooth * inv, scaled)
    return scaled.astype(np.float32)


def _rope_tables(pos0):
    inv = _llama3_inv_freq()
    pos = np.arange(pos0, pos0 + TPC, dtype=np.float32)
    fr = pos[None, :] * np.concatenate([inv, inv])[:, None]
    cos = np.cos(fr).astype(np.float32)
    sin = np.sin(fr).astype(np.float32)
    sf = sin.copy()
    sf[:64] = -sf[:64]
    return cos, sf


def _tile_w(w):
    I, O = w.shape
    t = w.reshape(I // 256, 2, 128, O // 256, 256)
    return np.ascontiguousarray(t.transpose(3, 0, 2, 1, 4).reshape(O // 256, I // 256, 128, 512))


def _ramp_consts():
    j = np.arange(128)[:, None]
    k = np.arange(128)[None, :]
    RA = ((j == 0) | ((k >= j) & (j >= 1))).astype(ml_dtypes.bfloat16)
    q = np.arange(512)[None, :]
    RBs = []
    for d in range(4):
        delta = 128 * d
        RB = np.where(j == 0, -RAMP_C * 128.0 * (q < delta),
                      np.where(j >= 1, -RAMP_C * (q < j + delta), 0.0))
        RBs.append(RB)
    return RA, np.stack(RBs).astype(ml_dtypes.bfloat16)


def _prep_inputs(inp):
    g = {}
    g["wq"] = _tile_w(np.asarray(inp["L0_attn_wq"]))
    g["wk"] = _tile_w(np.asarray(inp["L0_attn_wk"]))
    g["wv"] = _tile_w(np.asarray(inp["L0_attn_wv"]))
    g["wo"] = _tile_w(np.asarray(inp["L0_attn_wo"]))
    g["wt"] = _tile_w(np.ascontiguousarray(np.asarray(inp["L0_ffn_weight"]).T))
    A = np.asarray(inp["L0_ffn_lora_A"])
    Acat = np.ascontiguousarray(A.transpose(1, 0, 2).reshape(D, E * R))
    g["acat"] = np.ascontiguousarray(
        Acat.reshape(8, 4, 128, 128).transpose(0, 2, 1, 3).reshape(8, 128, 512))
    g["bcat"] = np.ascontiguousarray(np.asarray(inp["L0_ffn_lora_B"]).reshape(E * R, D))
    g["gw1t"] = np.ascontiguousarray(
        np.asarray(inp["L0_ffn_gw1"]).T.reshape(4, 8, 128, 32).transpose(0, 2, 1, 3)
        .reshape(4, 128, 256))
    g["gw2t"] = np.ascontiguousarray(np.asarray(inp["L0_ffn_gw2"]).T)
    g["gb1"] = np.ascontiguousarray(np.asarray(inp["L0_ffn_gb1"]).reshape(1, NH))
    g["lng"] = np.ascontiguousarray(np.asarray(inp["L0_ffn_ln_gamma"]).reshape(1, NH))
    g["lnb"] = np.ascontiguousarray(np.asarray(inp["L0_ffn_ln_beta"]).reshape(1, NH))
    g["gb2"] = np.ascontiguousarray(np.asarray(inp["L0_ffn_gb2"]).reshape(1, E))

    g["ident"] = np.eye(128, dtype=ml_dtypes.bfloat16)
    g["identf"] = np.eye(128, dtype=np.float32)
    RA, RB = _ramp_consts()
    g["rampa"] = RA
    g["rampb"] = RB
    e16 = np.zeros((E, E * R), dtype=ml_dtypes.bfloat16)
    for e in range(E):
        e16[e, e * R:(e + 1) * R] = ml_dtypes.bfloat16(1.0 / R)
    g["e16"] = e16
    g["onescol"] = np.ones((128, 1), dtype=ml_dtypes.bfloat16)
    g["onesrow"] = np.ones((1, 128), dtype=np.float32)
    g["negrow"] = np.full((1, 128), -1.0, dtype=ml_dtypes.bfloat16)

    x = np.asarray(inp["L0_attn_x"])
    scale = 1.0 / np.sqrt(HD)
    maps = []
    for c in range(NCORE):
        b, blkk = c // 4, c % 4
        pos0 = blkk * TPC
        cos, sf = _rope_tables(pos0)
        m = dict(g)
        m["x"] = np.ascontiguousarray(x[b, pos0:pos0 + TPC])
        m["cosq"] = np.ascontiguousarray(cos * scale)
        m["sfq"] = np.ascontiguousarray(sf * scale)
        m["cosk"] = cos
        m["sfk"] = sf
        m["rank"] = np.array([[c]], dtype=np.int32)
        maps.append(m)
    return maps



class Sem:
    def __init__(self, ctx, nc, name):
        self.h = ctx.enter_context(nc.semaphore(name))
        self.n = 0

    def inc(self, instr, k=1):
        instr.then_inc(self.h, k)
        self.n += k
        return self.n

    def dma(self, instr):
        return self.inc(instr, 16)

    def wait(self, eng, val=None):
        v = self.n if val is None else val
        if v > 0:
            eng.wait_ge(self.h, v)


def build_program(debug=False, stop_phase=None, no_xpose=False, xp_simple=False,
                  ktiles=8):
    nc = bass.Bass("TRN2", target_bir_lowering=False)
    ctx = contextlib.ExitStack()

    class LazyAP(dict):
        def __init__(self, specs):
            super().__init__()
            self.specs = dict(specs)

        def __missing__(self, name):
            sh, dt = self.specs[name]
            v = nc.dram_tensor(name, sh, dt, kind="ExternalInput").ap()
            self[name] = v
            return v

    _specs = {}
    for name, shape, dt in [
        ("x", [TPC, D], F32),
        ("wq", [16, 16, 128, 512], F32), ("wk", [4, 16, 128, 512], F32),
        ("wv", [4, 16, 128, 512], F32), ("wo", [16, 16, 128, 512], F32),
        ("wt", [16, 16, 128, 512], F32),
        ("acat", [8, 128, 512], F32), ("bcat", [128, D], F32),
        ("gw1t", [4, 128, 256], F32), ("gw2t", [32, 8], F32),
        ("gb1", [1, NH], F32), ("lng", [1, NH], F32), ("lnb", [1, NH], F32),
        ("gb2", [1, E], F32),
        ("cosq", [128, TPC], F32), ("sfq", [128, TPC], F32),
        ("cosk", [128, TPC], F32), ("sfk", [128, TPC], F32),
        ("ident", [128, 128], BF16), ("identf", [128, 128], F32),
        ("rampa", [128, 128], BF16),
        ("rampb", [4, 128, 512], BF16), ("e16", [E, E * R], BF16),
        ("onescol", [128, 1], BF16), ("onesrow", [1, 128], F32),
        ("negrow", [1, 128], BF16), ("rank", [1, 1], I32),
    ]:
        _specs[name] = (shape, dt)
    ap = LazyAP(_specs)
    y_out = nc.dram_tensor("y", [TPC, D], F32, kind="ExternalOutput").ap()
    dbg = nc.dram_tensor("dbg", [128, 256], F32, kind="ExternalOutput").ap() if debug else None

    qkv_in = nc.dram_tensor("qkv_in", [6144, TPC], BF16)
    qkv_out = nc.dram_tensor("qkv_out", [4 * 6144, TPC], BF16)
    o_in = nc.dram_tensor("o_in", [TPC, T], BF16)
    o_out_a = nc.dram_tensor("o_out_a", [4 * 512, T], BF16)
    o_out_b = nc.dram_tensor("o_out_b", [4 * 512, T], BF16)
    cnt_in = nc.dram_tensor("cnt_in", [1, 2 * E], F32)
    cnt_out = nc.dram_tensor("cnt_out", [1, 2 * E], F32)

    bigA = ctx.enter_context(nc.sbuf_tensor("bigA", [128, 32 * TPC], BF16))
    bigB = ctx.enter_context(nc.sbuf_tensor("bigB", [128, 32 * TPC], BF16))
    C = {}
    for n, sh, dt in [("ident", [128, 128], BF16), ("identf", [128, 128], F32),
                      ("rampa", [128, 128], BF16),
                      ("rampb", [128, 4 * 512], BF16), ("e16", [E, E * R], BF16),
                      ("onescol", [128, 1], BF16), ("onesrow", [1, 128], F32),
                      ("negrow", [1, 128], BF16), ("gw2t", [32, 8], BF16),
                      ("gw2tf", [32, 8], F32),
                      ("gb1", [1, NH], F32), ("lng", [1, NH], F32), ("lnb", [1, NH], F32),
                      ("gb2", [1, E], F32)]:
        C[n] = ctx.enter_context(nc.sbuf_tensor("c_" + n, sh, dt))
    wslot = ctx.enter_context(nc.sbuf_tensor("wslot", [128, NSLOT * 256], BF16))
    gbc = ctx.enter_context(nc.sbuf_tensor("gbc", [128, 3 * NH + E], F32))
    small = ctx.enter_context(nc.sbuf_tensor("small", [128, 32], F32))
    rkt = ctx.enter_context(nc.sbuf_tensor("rkt", [1, 1], I32))
    cpad = ctx.enter_context(nc.sbuf_tensor("cpad", [128, 2], F32))
    ucol = ctx.enter_context(nc.sbuf_tensor("ucol", [128, 16 * 8], F32))
    abc = ctx.enter_context(nc.sbuf_tensor("abc", [128, 2 * E], F32))
    negb = ctx.enter_context(nc.sbuf_tensor("negb", [128, 1], F32))
    phE = contextlib.ExitStack()
    scrA = phE.enter_context(nc.sbuf_tensor("scrA", [128, 1024], F32))
    scrB = phE.enter_context(nc.sbuf_tensor("scrB", [128, 1024], F32))
    pub = phE.enter_context(nc.sbuf_tensor("pub", [128, 2 * TPC], BF16))

    P = [ctx.enter_context(nc.psum_tensor(f"P{i}", [128, 1024], F32)) for i in range(3)]

    def bank(i):
        return P[i // 2][:, 512 * (i % 2):512 * (i % 2 + 1)]

    sW = Sem(ctx, nc, "sW")
    sWT = [Sem(ctx, nc, f"sWT{i}") for i in range(4)]
    sXs = [Sem(ctx, nc, "sX0"), Sem(ctx, nc, "sX1")]
    sP = Sem(ctx, nc, "sP")
    sPB = [Sem(ctx, nc, "sPB0"), Sem(ctx, nc, "sPB1")]
    sPO = [Sem(ctx, nc, "sPO0"), Sem(ctx, nc, "sPO1")]
    sC = Sem(ctx, nc, "sC")
    sT = Sem(ctx, nc, "sT")
    sU = Sem(ctx, nc, "sU")
    sE = Sem(ctx, nc, "sE")
    sV = Sem(ctx, nc, "sV")
    sS = Sem(ctx, nc, "sS")
    sSC = Sem(ctx, nc, "sSC")
    sEX = Sem(ctx, nc, "sEX")
    sAV = Sem(ctx, nc, "sAV")
    sUA = Sem(ctx, nc, "sUA")
    sOE = Sem(ctx, nc, "sOE")
    sG = Sem(ctx, nc, "sG")
    sOs = [Sem(ctx, nc, "sO0"), Sem(ctx, nc, "sO1")]

    blk = ctx.enter_context(nc.Block())

    @blk.gpsimd
    def _(g):
        g.memset(cpad[:, 0:1], LN_EPS)
        nc.const_aps.register(F32, LN_EPS, cpad[:, 0:1])
        sW.dma(g.dma_start(out=rkt[:], in_=ap["rank"]))
        for n in ["ident", "identf", "rampa", "e16", "onescol", "onesrow",
                  "negrow", "gb1", "lng", "lnb", "gb2"]:
            sW.dma(g.dma_start(out=C[n][:], in_=ap[n]))
        sW.dma(g.dma_start(out=C["gw2tf"][:], in_=ap["gw2t"]))
        for d in range(4):
            sW.dma(g.dma_start(out=C["rampb"][:, 512 * d:512 * (d + 1)], in_=ap["rampb"][d]))
    W_CONST = sW.n

    @blk.tensor
    def _(t):
        t.wait_ge(sW.h, W_CONST)
        t.matmul(P[0][:, 0:NH], C["onesrow"][:], C["gb1"][:], start=True, stop=True)
        t.matmul(P[0][:, NH:2 * NH], C["onesrow"][:], C["lng"][:], start=True, stop=True)
        t.matmul(P[0][:, 2 * NH:3 * NH], C["onesrow"][:], C["lnb"][:], start=True, stop=True)
        sT.inc(t.matmul(P[0][:, 3 * NH:3 * NH + E], C["onesrow"][:], C["gb2"][:],
                        start=True, stop=True))

    @blk.vector
    def _(v):
        sT.wait(v)
        v.tensor_copy(C["gw2t"][:], C["gw2tf"][:])
        sV.inc(v.tensor_copy(gbc[:, 0:3 * NH + E], P[0][:, 0:3 * NH + E]))

    ph1 = contextlib.ExitStack()
    if stop_phase == 0:
        def emit_stop0():
            sD = Sem(ctx, nc, "sD")

            @blk.gpsimd
            def _(g):
                for s in [sW, sT, sV]:
                    s.wait(g)
                sD.dma(g.dma_start(out=y_out[0:128, 0:104], in_=gbc[:, 0:104]))
                g.wait_ge(sD.h, sD.n)
            phE.close()
            ctx.close()
            return nc
        return emit_stop0()
    xbuf = ph1.enter_context(nc.sbuf_tensor("xbuf", [128, 2 * D], F32))
    xn_tm = ph1.enter_context(nc.sbuf_tensor("xn_tm", [128, 2 * D], BF16))
    phb1a = ph1.enter_context(nc.psum_tensor("phb1a", [128, 512], BF16))
    phb1b = ph1.enter_context(nc.psum_tensor("phb1b", [128, 512], BF16))
    st = small

    @blk.sync
    def _(s):
        for i in range(2):
            sXs[i].dma(s.dma_start(out=xbuf[:, D * i:D * (i + 1)],
                                   in_=ap["x"][128 * i:128 * (i + 1), :]))

    LN_S0 = sS.n
    LN_T0 = sT.n
    for i in range(ktiles):
        sl = D * (i % 2)
        sb = 16 * (i % 2)

        @blk.vector
        def _(v, i=i, sl=sl, sb=sb):
            v.wait_ge(sXs[i % 2].h, 16 * (i // 2 + 1))
            if i >= 2:
                v.wait_ge(sS.h, LN_S0 + 3 * (i - 1))
            def ve(instr):
                sV.inc(instr)
                v.wait_ge(sV.h, sV.n)
            v.tensor_reduce(st[:, sb:sb + 1], xbuf[:, sl:sl + D], AX.X, OP.add)
            v.tensor_tensor(scrA[:, 0:1024], xbuf[:, sl:sl + 1024],
                            xbuf[:, sl:sl + 1024], OP.mult)
            ve(v.tensor_tensor(scrB[:, 0:1024], xbuf[:, sl + 1024:sl + 2048],
                               xbuf[:, sl + 1024:sl + 2048], OP.mult))
            v.tensor_reduce(st[:, sb + 1:sb + 2], scrA[:, 0:1024], AX.X, OP.add)
            ve(v.tensor_reduce(st[:, sb + 6:sb + 7], scrB[:, 0:1024], AX.X, OP.add))
            v.tensor_tensor(scrA[:, 0:1024], xbuf[:, sl + 2048:sl + 3072],
                            xbuf[:, sl + 2048:sl + 3072], OP.mult)
            ve(v.tensor_tensor(scrB[:, 0:1024], xbuf[:, sl + 3072:sl + 4096],
                               xbuf[:, sl + 3072:sl + 4096], OP.mult))
            ve(v.tensor_tensor(st[:, sb + 1:sb + 2], st[:, sb + 1:sb + 2],
                               st[:, sb + 6:sb + 7], OP.add))
            v.tensor_reduce(st[:, sb + 6:sb + 7], scrA[:, 0:1024], AX.X, OP.add)
            ve(v.tensor_reduce(st[:, sb + 7:sb + 8], scrB[:, 0:1024], AX.X, OP.add))
            ve(v.tensor_tensor(st[:, sb + 6:sb + 7], st[:, sb + 6:sb + 7],
                               st[:, sb + 7:sb + 8], OP.add))
            ve(v.tensor_tensor(st[:, sb + 1:sb + 2], st[:, sb + 1:sb + 2],
                               st[:, sb + 6:sb + 7], OP.add))
            ve(v.tensor_scalar(out=st[:, sb + 2:sb + 3], in0=st[:, sb:sb + 1],
                               scalar1=1.0 / D, scalar2=None, op0=OP.mult))
            ve(v.tensor_tensor(st[:, sb + 3:sb + 4], st[:, sb + 2:sb + 3],
                               st[:, sb + 2:sb + 3], OP.mult))
            ve(v.tensor_scalar(out=st[:, sb + 3:sb + 4], in0=st[:, sb + 1:sb + 2],
                               scalar1=1.0 / D, scalar2=st[:, sb + 3:sb + 4],
                               op0=OP.mult, op1=OP.subtract))

        @blk.scalar
        def _(sc, sb=sb):
            sV.wait(sc)
            sS.inc(sc.activation(st[:, sb + 4:sb + 5], st[:, sb + 3:sb + 4],
                                 (AF.Identity if NOLN else AF.Ln), bias=LN_EPS, scale=1.0))
            sc.wait_ge(sS.h, sS.n)
            sS.inc(sc.activation(st[:, sb + 4:sb + 5], st[:, sb + 4:sb + 5],
                                 AF.Exp, bias=0.0, scale=-0.5))

        @blk.vector
        def _(v, sb=sb):
            sS.wait(v)
            sV.inc(v.tensor_tensor(st[:, sb + 5:sb + 6], st[:, sb + 2:sb + 3],
                                   st[:, sb + 4:sb + 5], OP.mult))
            v.wait_ge(sV.h, sV.n)
            sV.inc(v.tensor_scalar(out=st[:, sb + 5:sb + 6], in0=st[:, sb + 5:sb + 6],
                                   scalar1=-1.0, scalar2=None, op0=OP.mult))

        @blk.scalar
        def _(sc, i=i, sl=sl, sb=sb):
            sV.wait(sc)
            if i >= 2 and not no_xpose:
                sc.wait_ge(sT.h, LN_T0 + 8 * (i - 1))
            sS.inc(sc.activation(xn_tm[:, sl:sl + D], xbuf[:, sl:sl + D], AF.Identity,
                                 bias=st[:, sb + 5:sb + 6], scale=st[:, sb + 4:sb + 5]))

        @blk.sync
        def _(s, i=i, sl=sl):
            if i + 2 < ktiles:
                sS.wait(s)
                sXs[i % 2].dma(s.dma_start(out=xbuf[:, sl:sl + D],
                                           in_=ap["x"][128 * (i + 2):128 * (i + 3), :]))

        for fg in range(8 if not no_xpose else 0):
            reg = (phb1a if fg % 2 == 0 else phb1b)[:, 0:512]

            @blk.tensor
            def _(t, i=i, sl=sl, fg=fg, reg=reg):
                if fg == 0:
                    t.wait_ge(sS.h, LN_S0 + 3 * (i + 1))
                if fg >= 2:
                    t.wait_ge(sV.h, sV.n)
                last = None
                for j in range(4):
                    fc = 4 * fg + j
                    last = t.transpose(reg[:, 128 * j:128 * (j + 1)],
                                       xn_tm[:, sl + 128 * fc:sl + 128 * (fc + 1)],
                                       C["ident"][:])
                sT.inc(last)

            @blk.vector
            def _(v, i=i, fg=fg, reg=reg):
                sT.wait(v)
                out = bigA[:].rearrange("p (f c) -> p f c", c=TPC)[
                    :, 4 * fg:4 * fg + 4, 128 * i:128 * (i + 1)]
                sV.inc(v.tensor_copy(out, reg[:].rearrange("p (f c) -> p f c", c=128)))

    V_XN = sV.n

    def emit_stop_dump(srcs, stacks=()):
        sD = Sem(ctx, nc, "sD")

        @blk.gpsimd
        def _(g):
            for s in [sW, sP, sC, sT, sU, sE, sV, sS,
                      sSC, sEX, sAV, sUA, sOE, sG] + sWT + sXs + sPB + sPO + sOs:
                s.wait(g)
            for i, (src_ap, nr, ncol) in enumerate(srcs):
                sD.dma(g.dma_start(out=y_out[128 * i:128 * i + nr, 0:ncol], in_=src_ap))
            g.wait_ge(sD.h, sD.n)
        for stk in stacks:
            stk.close()
        ctx.close()
        return nc

    if stop_phase != 1:
        ph1.close()
    if stop_phase == 1:
        if no_xpose:
            return emit_stop_dump([(xn_tm[:, 0:D], 128, D), (xn_tm[:, D:2 * D], 128, D)],
                                  stacks=(ph1, phE))
        return emit_stop_dump([(bigA[:, TPC * i:TPC * (i + 1)], 128, TPC) for i in range(8)],
                              stacks=(ph1, phE))

    ph2 = contextlib.ExitStack()
    gw1t_sb = ph2.enter_context(nc.sbuf_tensor("gw1t_sb", [128, 32 * 32], BF16))
    htv = ph2.enter_context(nc.sbuf_tensor("htv", [32, TPC], BF16))
    hs = ph2.enter_context(nc.sbuf_tensor("hs", [128, 2 * NH], F32))
    g0 = ph2.enter_context(nc.sbuf_tensor("g0", [128, E], F32))
    mbuf = ph2.enter_context(nc.sbuf_tensor("mbuf", [128, 2 * E], BF16))
    cnt_sb = ph2.enter_context(nc.sbuf_tensor("cnt_sb", [8, 2], F32))

    @blk.gpsimd
    def _(g):
        sV.wait(g)
        sS.wait(g)
        sT.wait(g)
        for i in range(4):
            sW.dma(g.dma_start(out=gw1t_sb[:, 256 * i:256 * (i + 1)], in_=ap["gw1t"][i]))
    W_GW1 = sW.n

    for tc in range(8):
        @blk.tensor
        def _(t, tc=tc):
            if tc == 0:
                t.wait_ge(sW.h, W_GW1)
                t.wait_ge(sV.h, V_XN)
            else:
                sV.wait(t)
            last = None
            for i in range(32):
                last = t.matmul(P[0][:, 0:NH],
                                bigA[:, TPC * i + 128 * tc:TPC * i + 128 * (tc + 1)],
                                gw1t_sb[:, NH * i:NH * (i + 1)],
                                start=(i == 0), stop=(i == 31))
            sT.inc(last)

        @blk.vector
        def _(v, tc=tc):
            sT.wait(v)
            sS.wait(v)
            sV.inc(v.tensor_tensor(hs[:, 0:NH], P[0][:, 0:NH], gbc[:, 0:NH], OP.add))
            v.wait_ge(sV.h, sV.n)
            v.tensor_reduce(st[:, 8:9], hs[:, 0:NH], AX.X, OP.add)
            sV.inc(v.tensor_tensor(hs[:, NH:2 * NH], hs[:, 0:NH], hs[:, 0:NH], OP.mult))
            v.wait_ge(sV.h, sV.n)
            sV.inc(v.tensor_reduce(st[:, 9:10], hs[:, NH:2 * NH], AX.X, OP.add))
            v.wait_ge(sV.h, sV.n)
            sV.inc(v.tensor_scalar(out=st[:, 10:11], in0=st[:, 8:9], scalar1=1.0 / NH,
                                   scalar2=None, op0=OP.mult))
            v.wait_ge(sV.h, sV.n)
            sV.inc(v.tensor_tensor(st[:, 11:12], st[:, 10:11], st[:, 10:11], OP.mult))
            v.wait_ge(sV.h, sV.n)
            sV.inc(v.tensor_scalar(out=st[:, 11:12], in0=st[:, 9:10], scalar1=1.0 / NH,
                                   scalar2=st[:, 11:12], op0=OP.mult, op1=OP.subtract))

        @blk.scalar
        def _(sc):
            sV.wait(sc)
            sS.inc(sc.activation(st[:, 12:13], st[:, 11:12], (AF.Identity if NOLN else AF.Ln), bias=LN_EPS, scale=1.0))
            sc.wait_ge(sS.h, sS.n)
            sS.inc(sc.activation(st[:, 12:13], st[:, 12:13], AF.Exp, bias=0.0, scale=-0.5))

        @blk.vector
        def _(v, tc=tc):
            sS.wait(v)
            sV.inc(v.tensor_scalar(out=hs[:, 0:NH], in0=hs[:, 0:NH], scalar1=st[:, 10:11],
                                   scalar2=st[:, 12:13], op0=OP.subtract, op1=OP.mult))
            v.wait_ge(sV.h, sV.n)
            sV.inc(v.tensor_tensor(hs[:, 0:NH], hs[:, 0:NH], gbc[:, NH:2 * NH], OP.mult))
            v.wait_ge(sV.h, sV.n)
            sV.inc(v.tensor_tensor(hs[:, 0:NH], hs[:, 0:NH], gbc[:, 2 * NH:3 * NH], OP.add))
            v.wait_ge(sV.h, sV.n)
            sV.inc(v.tensor_scalar(out=hs[:, NH:2 * NH], in0=hs[:, 0:NH], scalar1=0.0,
                                   scalar2=None, op0=OP.max))

        @blk.tensor
        def _(t, tc=tc):
            sV.wait(t)
            sT.inc(t.transpose(P[1][0:NH, 0:128], hs[:, NH:2 * NH], C["identf"][:]))

        @blk.vector
        def _(v, tc=tc):
            sT.wait(v)
            sV.inc(v.tensor_copy(htv[:, 128 * tc:128 * (tc + 1)], P[1][0:NH, 0:128]))

        @blk.tensor
        def _(t, tc=tc):
            sV.wait(t)
            sT.inc(t.matmul(P[1][:, 512:512 + E], htv[:, 128 * tc:128 * (tc + 1)],
                            C["gw2t"][:], start=True, stop=True))

        @blk.vector
        def _(v, tc=tc):
            sT.wait(v)
            sV.inc(v.tensor_tensor(g0[:, 0:E], P[1][:, 512:512 + E],
                                   gbc[:, 3 * NH:3 * NH + E], OP.add))
            v.wait_ge(sV.h, sV.n)
            sV.inc(v.tensor_reduce(st[:, 16:17], g0[:, 0:E], AX.X, OP.max))
            v.wait_ge(sV.h, sV.n)
            sV.inc(v.tensor_scalar(out=mbuf[:, 0:E], in0=g0[:, 0:E], scalar1=st[:, 16:17],
                                   scalar2=None, op0=OP.is_equal))
            v.wait_ge(sV.h, sV.n)
            sV.inc(v.scalar_tensor_tensor(out=hs[:, 0:E], in0=mbuf[:, 0:E], scalar=-1e9,
                                          in1=g0[:, 0:E], op0=OP.mult, op1=OP.add))
            v.wait_ge(sV.h, sV.n)
            sV.inc(v.tensor_reduce(st[:, 17:18], hs[:, 0:E], AX.X, OP.max))
            v.wait_ge(sV.h, sV.n)
            sV.inc(v.tensor_scalar(out=mbuf[:, E:2 * E], in0=hs[:, 0:E], scalar1=st[:, 17:18],
                                   scalar2=None, op0=OP.is_equal))
            sV.inc(v.tensor_scalar(out=st[:, 18:19], in0=st[:, 16:17], scalar1=-1.0,
                                   scalar2=None, op0=OP.mult))
            sV.inc(v.tensor_scalar(out=st[:, 19:20], in0=st[:, 17:18], scalar1=-1.0,
                                   scalar2=None, op0=OP.mult))

        @blk.scalar
        def _(sc):
            sV.wait(sc)
            sS.inc(sc.activation(st[:, 20:21], st[:, 16:17], AF.Sigmoid,
                                 bias=st[:, 19:20], scale=1.0))
            sS.inc(sc.activation(st[:, 21:22], st[:, 17:18], AF.Sigmoid,
                                 bias=st[:, 18:19], scale=1.0))

        @blk.vector
        def _(v, tc=tc):
            sS.wait(v)
            sV.inc(v.tensor_scalar(out=ucol[:, 16 * tc:16 * tc + E], in0=mbuf[:, 0:E],
                                   scalar1=st[:, 20:21], scalar2=None, op0=OP.mult))
            sV.inc(v.tensor_scalar(out=ucol[:, 16 * tc + E:16 * tc + 2 * E],
                                   in0=mbuf[:, E:2 * E],
                                   scalar1=st[:, 21:22], scalar2=None, op0=OP.mult))

        @blk.tensor
        def _(t, tc=tc):
            sV.wait(t)
            t.matmul(P[2][0:E, 0:1], mbuf[:, 0:E], C["onescol"][:],
                     start=(tc == 0), stop=(tc == 7))
            last = t.matmul(P[2][0:E, 512:513], mbuf[:, E:2 * E], C["onescol"][:],
                            start=(tc == 0), stop=(tc == 7))
            if tc == 7:
                sT.inc(last)

    @blk.vector
    def _(v):
        sT.wait(v)
        v.tensor_copy(cnt_sb[:, 0:1], P[2][0:E, 0:1])
        sV.inc(v.tensor_copy(cnt_sb[:, 1:2], P[2][0:E, 512:513]))

    @blk.sync
    def _(s):
        sV.wait(s)
        sP.dma(s.dma_start(out=cnt_in.ap()[0:1, 0:E], in_=cnt_sb[0:E, 0:1]))
        sP.dma(s.dma_start(out=cnt_in.ap()[0:1, E:2 * E], in_=cnt_sb[0:E, 1:2]))

    @blk.gpsimd
    def _(g):
        sP.wait(g)
        sC.inc(g.collective_compute("AllReduce", OP.add, replica_groups=GROUPS_ALL,
                                    ins=[cnt_in.ap()], outs=[cnt_out.ap()]))
    C_CNT = sC.n
    if stop_phase == 2:
        srcs = [(ucol[:, 0:128], 128, 128), (gbc[:, 0:104], 128, 104),
                (cnt_out.ap(), 1, 16), (hs[:, 0:2 * NH], 128, 2 * NH),
                (htv[0:32, 0:TPC], 32, TPC), (g0[:, 0:E], 128, E),
                (gw1t_sb[:, 0:1024], 128, 1024)]
        ph2.close()
        return emit_stop_dump(srcs, stacks=(phE,))
    ph2.close()

    state = {"gi": 0}

    def emit_proj(wdram, n_og, evac_fn, inject_fn=None, first_gate=None,
                  bank_fn=None, serial4=False, iperm=None, i_gate=None):
        bk = bank_fn if bank_fn is not None else bank
        e_base = sE.n
        gi0 = state["gi"]
        for og in range(n_og):
            @blk.gpsimd
            def _(g, og=og):
                for ip in range(16):
                    gp_ = (gi0 // 2) + og * 16 + ip
                    icp = (2 * ip) if iperm is None else iperm[2 * ip]
                    if gp_ >= 4:
                        g.wait_ge(sU.h, 2 * (gp_ - 3))
                    sWT[gp_ % 4].dma(g.dma_start(
                        out=wslot[:, 512 * (gp_ % 4):512 * (gp_ % 4 + 1)],
                        in_=wdram[og, icp // 2]))

            @blk.tensor
            def _(t, og=og):
                if og == 0 and first_gate is not None:
                    first_gate(t)
                if serial4:
                    if e_base + 4 * og > 0:
                        t.wait_ge(sE.h, e_base + 4 * og)
                    b0 = 0
                else:
                    if e_base + 4 * (og - 1) > 0:
                        t.wait_ge(sE.h, e_base + 4 * (og - 1))
                    b0 = 4 * (og % 2)
                for i in range(32):
                    gi = gi0 + og * 32 + i
                    ic = i if iperm is None else iperm[i]
                    gp_ = gi // 2
                    t.wait_ge(sWT[gp_ % 4].h, 16 * (gp_ // 4 + 1))
                    if i_gate is not None:
                        i_gate(t, i)
                    off = 512 * (gp_ % 4) + 256 * (gi % 2)
                    last = None
                    for m in range(2):
                        for tb in range(2):
                            last = t.matmul(
                                bk(b0 + 2 * m + tb),
                                wslot[:, off + 128 * m:off + 128 * (m + 1)],
                                bigA[:, TPC * ic + 512 * tb:TPC * ic + 512 * (tb + 1)],
                                start=(i == 0),
                                stop=(i == 31 and inject_fn is None))
                    sU.inc(last)
                if inject_fn is not None:
                    last = None
                    for m in range(2):
                        for tb in range(2):
                            lh, rh = inject_fn(og, m, tb)
                            last = t.matmul(bk(b0 + 2 * m + tb), lh, rh,
                                            start=False, stop=True)
                    sT.inc(last)
            u_mark = sU.n
            t_mark = sT.n
            b0e = 0 if serial4 else 4 * (og % 2)
            for m in range(2):
                for tb in range(2):
                    evac_fn(og, m, tb, bk(b0e + 2 * m + tb), u_mark, t_mark)
        state["gi"] = gi0 + n_og * 32

    phf = contextlib.ExitStack()
    PHF = phf.enter_context(nc.psum_tensor("PHF", [128, 1024], F32))

    def bank8(i):
        return bank(i) if i < 6 else PHF[:, 512 * (i - 6):512 * (i - 5)]

    ph3 = contextlib.ExitStack()
    rope = {}
    for n in ["cosq", "sfq", "cosk", "sfk"]:
        rope[n] = ph3.enter_context(nc.sbuf_tensor(n + "_sb", [128, TPC], F32))

    @blk.gpsimd
    def _(g):
        sV.wait(g)
        sS.wait(g)
        sT.wait(g)
        for n in ["cosq", "sfq", "cosk", "sfk"]:
            sW.dma(g.dma_start(out=rope[n][:], in_=ap[n]))
    W_ROPE = sW.n

    pub_state = {"n": 0, "slot": [0, 0]}

    def mk_qkv_evac(cosn, sfn, row0, _unused):
        def evac(og, m, tb, ps, u_mark, t_mark):
            mg = og * 2 + m
            slot = mg % 2

            @blk.vector
            def _(v, og=og, m=m, tb=tb, ps=ps, u_mark=u_mark, mg=mg, slot=slot):
                v.wait_ge(sU.h, u_mark)
                if tb == 0 and pub_state["slot"][slot] > 0:
                    v.wait_ge(sPB[slot].h, 16 * pub_state["slot"][slot])
                dst = pub[:, TPC * slot + 512 * tb:TPC * slot + 512 * (tb + 1)]
                if cosn is None:
                    sE.inc(v.tensor_copy(dst, ps))
                else:
                    cs = rope[cosn]
                    sf = rope[sfn]
                    v.wait_ge(sE.h, sE.n)
                    v.tensor_tensor(scrA[:, 0:512], ps, cs[:, 512 * tb:512 * (tb + 1)], OP.mult)
                    v.tensor_tensor(scrB[0:64, 0:512], ps[64:128, :],
                                    sf[0:64, 512 * tb:512 * (tb + 1)], OP.mult)
                    sV.inc(v.tensor_tensor(scrB[64:128, 0:512], ps[0:64, :],
                                           sf[64:128, 512 * tb:512 * (tb + 1)], OP.mult))
                    v.wait_ge(sV.h, sV.n)
                    sE.inc(v.tensor_tensor(dst, scrA[:, 0:512], scrB[:, 0:512], OP.add))

            if tb == 1:
                @blk.sync
                def _(s, mg=mg, slot=slot):
                    s.wait_ge(sE.h, sE.n)
                    sPB[slot].dma(s.dma_start(
                        out=qkv_in.ap()[row0 + 128 * mg:row0 + 128 * (mg + 1), :],
                        in_=pub[:, TPC * slot:TPC * (slot + 1)]))
                pub_state["slot"][slot] += 1
                pub_state["n"] += 1
        return evac

    def gate_rope(t):
        t.wait_ge(sW.h, W_ROPE)
        sV.wait(t)

    emit_proj(ap["wk"], 4, mk_qkv_evac("cosk", "sfk", 4096, None), first_gate=gate_rope,
              bank_fn=bank8)
    emit_proj(ap["wv"], 4, mk_qkv_evac(None, None, 5120, None), bank_fn=bank8)
    emit_proj(ap["wq"], 16, mk_qkv_evac("cosq", "sfq", 0, None), bank_fn=bank8)

    @blk.gpsimd
    def _(g):
        for slot in range(2):
            g.wait_ge(sPB[slot].h, 16 * pub_state["slot"][slot])
        sC.inc(g.collective_compute("AllGather", OP.bypass, replica_groups=GROUPS_HALF,
                                    ins=[qkv_in.ap()], outs=[qkv_out.ap()]))
    C_QKV = sC.n
    if stop_phase == 3:
        @blk.gpsimd
        def _(g):
            g.wait_ge(sC.h, C_QKV)
        srcs = [(qkv_out.ap()[6144 * 0 + 128 * 0:6144 * 0 + 128, 0:1024], 128, 1024),
                (qkv_out.ap()[6144 * 1 + 4096:6144 * 1 + 4224, 0:1024], 128, 1024),
                (qkv_out.ap()[6144 * 2 + 5120:6144 * 2 + 5248, 0:1024], 128, 1024),
                (qkv_out.ap()[6144 * 3 + 2048:6144 * 3 + 2176, 0:1024], 128, 1024)]
        ph3.close()
        return emit_stop_dump(srcs, stacks=(phf, phE))
    ph3.close()

    ph4 = contextlib.ExitStack()
    acat_sb = ph4.enter_context(nc.sbuf_tensor("acat_sb", [128, 8 * 512], BF16))
    bcat_sb = ph4.enter_context(nc.sbuf_tensor("bcat_sb", [128, D], BF16))
    tmp_sb = ph4.enter_context(nc.sbuf_tensor("tmp_sb", [128, TPC], BF16))
    tmps_sb = ph4.enter_context(nc.sbuf_tensor("tmps_sb", [128, TPC], BF16))
    alw = ph4.enter_context(nc.sbuf_tensor("alw", [1, 2 * E], F32))
    wg8 = ph4.enter_context(nc.sbuf_tensor("wg8", [128, 8 * 8], F32))
    wgT = ph4.enter_context(nc.sbuf_tensor("wgT", [8, TPC], BF16))

    @blk.gpsimd
    def _(g):
        sE.wait(g)
        sV.wait(g)
        for i in range(8):
            sW.dma(g.dma_start(out=acat_sb[:, 512 * i:512 * (i + 1)], in_=ap["acat"][i]))
        for i in range(8):
            sW.dma(g.dma_start(out=bcat_sb[:, 512 * i:512 * (i + 1)],
                               in_=ap["bcat"][:, 512 * i:512 * (i + 1)]))
    W_AB = sW.n

    @blk.tensor
    def _(t):
        t.wait_ge(sW.h, W_AB)
        sE.wait(t)
        last = None
        for i in range(32):
            lh = acat_sb[:, 512 * (i // 4) + 128 * (i % 4):512 * (i // 4) + 128 * (i % 4 + 1)]
            for tb in range(2):
                last = t.matmul(bank(tb), lh,
                                bigA[:, TPC * i + 512 * tb:TPC * i + 512 * (tb + 1)],
                                start=(i == 0), stop=(i == 31))
        sT.inc(last)

    @blk.scalar
    def _(sc):
        sT.wait(sc)
        sS.inc(sc.activation(tmp_sb[:, 0:512], bank(0), AF.Identity, bias=0.0, scale=1.0))
        sS.inc(sc.activation(tmp_sb[:, 512:1024], bank(1), AF.Identity, bias=0.0, scale=1.0))

    @blk.gpsimd
    def _(g):
        g.wait_ge(sC.h, C_CNT)
        sE.wait(g)
        sW.dma(g.dma_start(out=alw[:], in_=cnt_out.ap()))
    W_CNTRD = sW.n

    @blk.vector
    def _(v):
        v.wait_ge(sW.h, W_CNTRD)
        sV.inc(v.tensor_scalar(out=alw[:], in0=alw[:], scalar1=CAPACITY, scalar2=None,
                               op0=OP.is_le))

    @blk.tensor
    def _(t):
        sV.wait(t)
        t.matmul(P[1][:, 0:E], C["onesrow"][:], alw[0:1, 0:E], start=True, stop=True)
        sT.inc(t.matmul(P[1][:, E:2 * E], C["onesrow"][:], alw[0:1, E:2 * E],
                        start=True, stop=True))

    @blk.vector
    def _(v):
        sT.wait(v)
        sV.inc(v.tensor_copy(abc[:, 0:2 * E], P[1][:, 0:2 * E]))

    for tc in range(8):
        @blk.vector
        def _(v, tc=tc):
            v.wait_ge(sV.h, sV.n)
            sV.inc(v.tensor_tensor(scrA[:, 0:E], ucol[:, 16 * tc:16 * tc + E],
                                   abc[:, 0:E], OP.mult))
            sV.inc(v.tensor_tensor(scrA[:, E:2 * E], ucol[:, 16 * tc + E:16 * tc + 2 * E],
                                   abc[:, E:2 * E], OP.mult))
            v.wait_ge(sV.h, sV.n)
            sV.inc(v.tensor_tensor(wg8[:, E * tc:E * (tc + 1)], scrA[:, 0:E],
                                   scrA[:, E:2 * E], OP.add))

        @blk.tensor
        def _(t, tc=tc):
            sV.wait(t)
            sT.inc(t.transpose(P[1][0:E, 128:256], wg8[:, E * tc:E * (tc + 1)],
                               C["identf"][:]))

        @blk.vector
        def _(v, tc=tc):
            sT.wait(v)
            sV.inc(v.tensor_copy(wgT[:, 128 * tc:128 * (tc + 1)], P[1][0:E, 128:256]))

    @blk.tensor
    def _(t):
        sV.wait(t)
        t.matmul(P[1][:, 0:512], C["e16"][:], wgT[:, 0:512], start=True, stop=True)
        sT.inc(t.matmul(P[1][:, 512:1024], C["e16"][:], wgT[:, 512:1024],
                        start=True, stop=True))

    @blk.vector
    def _(v):
        sT.wait(v)
        sS.wait(v)
        sV.inc(v.tensor_tensor(tmps_sb[:, 0:512], tmp_sb[:, 0:512], P[1][:, 0:512], OP.mult))
        sV.inc(v.tensor_tensor(tmps_sb[:, 512:1024], tmp_sb[:, 512:1024],
                               P[1][:, 512:1024], OP.mult))
    V_TMPS = sV.n

    def ffn_inject(og, m, tb):
        oc = og * 2 + m
        return (bcat_sb[:, 128 * oc:128 * (oc + 1)],
                tmps_sb[:, 512 * tb:512 * (tb + 1)])

    def ffn_evac(og, m, tb, ps, u_mark, t_mark):
        oc = og * 2 + m

        @blk.scalar
        def _(sc, oc=oc, tb=tb, ps=ps, t_mark=t_mark):
            sc.wait_ge(sT.h, t_mark)
            sE.inc(sc.activation(bigB[:, TPC * oc + 512 * tb:TPC * oc + 512 * (tb + 1)],
                                 ps, AF.Silu, bias=0.0, scale=1.0))

    def gate_ffn(t):
        t.wait_ge(sV.h, V_TMPS)

    emit_proj(ap["wt"], 16, ffn_evac, inject_fn=ffn_inject, first_gate=gate_ffn,
              bank_fn=bank8)
    E_FFN = sE.n
    U_FFN = sU.n
    phf.close()
    if stop_phase == 4:
        srcs = [(bigB[:, TPC * i:TPC * (i + 1)], 128, TPC) for i in range(6)]
        srcs.append((tmps_sb[:, 0:1024], 128, 1024))
        srcs.append((wgT[0:8, 0:1024], 8, 1024))
        ph4.close()
        return emit_stop_dump(srcs, stacks=(phf, phE))
    ph4.close()
    phE.close()

    ph5 = contextlib.ExitStack()
    kta = ph5.enter_context(nc.sbuf_tensor("kta", [128, 2 * T], BF16))
    vtm = ph5.enter_context(nc.sbuf_tensor("vtm", [128, 2 * T], BF16))
    pT = ph5.enter_context(nc.sbuf_tensor("pT", [128, 3 * 1024], BF16))
    psacc = ph5.enter_context(nc.sbuf_tensor("psacc", [128, 512], F32))
    pbf = ph5.enter_context(nc.sbuf_tensor("pbf", [128, 512], BF16))
    usb = ph5.enter_context(nc.sbuf_tensor("usb", [128, 512], BF16))
    rz_sb = ph5.enter_context(nc.sbuf_tensor("rz_sb", [1, 512], F32))
    rzbc_sb = ph5.enter_context(nc.sbuf_tensor("rzbc_sb", [128, 512], F32))
    obuf = ph5.enter_context(nc.sbuf_tensor("obuf", [128, 2 * 512], BF16))
    bbf = ph5.enter_context(nc.sbuf_tensor("bbf", [1, 1], BF16))
    ph5a = contextlib.ExitStack()
    vta = ph5a.enter_context(nc.sbuf_tensor("vta", [128, 2 * T], BF16))
    q2b = vta
    phb5a = ph5a.enter_context(nc.psum_tensor("phb5a", [128, 512], BF16))
    phb5b = ph5a.enter_context(nc.psum_tensor("phb5b", [128, 512], BF16))

    @blk.gpsimd
    def _(g):
        g.wait_ge(sC.h, C_QKV)
        g.wait_ge(sU.h, U_FFN)
        g.wait_ge(sV.h, sV.n)
        sS.wait(g)
        sT.wait(g)
        rankv = nc.values_load(rkt[:], engines=[mybir.EngineType.Pool],
                               min_val=0, max_val=7, skip_runtime_bounds_check=True)
        qo = qkv_out.ap()
        for c in g.Switch(rankv, 8):
            hb0 = 1024 * (c % 4)
            kb0 = 256 * (c % 4)
            for hc in range(8):
                for r in range(4):
                    g.dma_start(out=bigA[:, 4096 * hc + 1024 * r:4096 * hc + 1024 * (r + 1)],
                                in_=qo[6144 * r + hb0 + 128 * hc:6144 * r + hb0 + 128 * (hc + 1), :]
                                ).then_inc(sG.h, 16)
            for k2 in range(2):
                for r in range(4):
                    g.dma_start(out=kta[:, 4096 * k2 + 1024 * r:4096 * k2 + 1024 * (r + 1)],
                                in_=qo[6144 * r + 4096 + kb0 + 128 * k2:
                                       6144 * r + 4096 + kb0 + 128 * (k2 + 1), :]
                                ).then_inc(sG.h, 16)
            for k2 in range(2):
                for r in range(4):
                    g.dma_start(out=vta[:, 4096 * k2 + 1024 * r:4096 * k2 + 1024 * (r + 1)],
                                in_=qo[6144 * r + 5120 + kb0 + 128 * k2:
                                       6144 * r + 5120 + kb0 + 128 * (k2 + 1), :]
                                ).then_inc(sG.h, 16)
    sG.n += 48 * 16
    G_ASM = sG.n

    VT_BASE = [0]
    VT_BASE[0] = sV.n
    for kc in range(32):
        reg = phb5a if kc % 2 == 0 else phb5b

        @blk.tensor
        def _(t, kc=kc, reg=reg):
            if kc == 0:
                t.wait_ge(sG.h, G_ASM)
                sE.wait(t)
                sS.wait(t)
            elif kc >= 2:
                v_need = VT_BASE[0] + kc - 1
                t.wait_ge(sV.h, v_need)
            last = None
            for k2 in range(2):
                last = t.transpose(reg[:, 128 * k2:128 * (k2 + 1)],
                                   vta[:, 4096 * k2 + 128 * kc:4096 * k2 + 128 * (kc + 1)],
                                   C["ident"][:])
            sT.inc(last)

        @blk.vector
        def _(v, kc=kc, reg=reg):
            sT.wait(v)
            sV.inc(v.tensor_copy(vtm[:, 256 * kc:256 * (kc + 1)], reg[:, 0:256]))


    @blk.vector
    def _(v):
        sV.inc(v.memset(st[:, 24:26], 0.0))
    for grp in range(10):
        src = (bigA if grp < 8 else kta)
        col = 4096 * grp if grp < 8 else 4096 * (grp - 8)

        @blk.vector
        def _(v, src=src, col=col, grp=grp):
            if grp == 0:
                v.wait_ge(sG.h, G_ASM)
            v.wait_ge(sV.h, sV.n)
            sV.inc(v.tensor_tensor(q2b[:, 0:4096], src[:, col:col + 4096],
                                   src[:, col:col + 4096], OP.mult))

        for half in range(2):
            @blk.tensor
            def _(t, grp=grp, half=half):
                sV.wait(t)
                last = None
                for s4 in range(4):
                    s8 = 4 * half + s4
                    last = t.matmul(bank(s4)[0:1, :],
                                    C["onescol"][:], q2b[:, 512 * s8:512 * (s8 + 1)],
                                    start=True, stop=True)
                sT.inc(last)

            @blk.vector
            def _(v, grp=grp, half=half):
                sT.wait(v)
                dst = st[:, 24:25] if grp < 8 else st[:, 25:26]
                for s4 in range(4):
                    v.wait_ge(sV.h, sV.n)
                    sV.inc(v.tensor_reduce(st[0:1, 26:27], bank(s4)[0:1, :], AX.X, OP.max))
                    v.wait_ge(sV.h, sV.n)
                    sV.inc(v.tensor_tensor(dst[0:1, :], dst[0:1, :], st[0:1, 26:27], OP.max))

    @blk.scalar
    def _(sc):
        sV.wait(sc)
        sS.inc(sc.activation(st[0:1, 27:28], st[0:1, 24:25], (AF.Identity if NOLN else AF.Ln), bias=LN_EPS, scale=1.0))
        sS.inc(sc.activation(st[0:1, 28:29], st[0:1, 25:26], (AF.Identity if NOLN else AF.Ln), bias=LN_EPS, scale=1.0))

    @blk.vector
    def _(v):
        sS.wait(v)
        sV.inc(v.tensor_tensor(st[0:1, 27:28], st[0:1, 27:28], st[0:1, 28:29], OP.add))

    @blk.scalar
    def _(sc):
        sV.wait(sc)
        sS.inc(sc.activation(st[0:1, 29:30], st[0:1, 27:28], AF.Exp, bias=0.0, scale=0.5))

    @blk.vector
    def _(v):
        sS.wait(v)
        sV.inc(v.tensor_copy(bbf[:], st[0:1, 29:30]))

    @blk.tensor
    def _(t):
        sV.wait(t)
        sT.inc(t.matmul(P[2][:, 0:1], C["negrow"][:], bbf[:], start=True, stop=True))

    @blk.vector
    def _(v):
        sT.wait(v)
        sV.inc(v.tensor_copy(negb[:], P[2][:, 0:1]))
    V_NEGB = sV.n
    ph5a.close()
    ph5b = contextlib.ExitStack()
    pzr = ph5b.enter_context(nc.psum_tensor("pzr", [128, 1024], F32))

    sTZ = Sem(ctx, nc, "sTZ")
    sTR = Sem(ctx, nc, "sTR")
    sZ1 = Sem(ctx, nc, "sZ1")
    sZ2 = Sem(ctx, nc, "sZ2")
    sZ3 = Sem(ctx, nc, "sZ3")

    blocks = []
    gp0 = 0
    for hb in range(8):
        for qb in range(8):
            blocks.append((hb, qb, 2 * qb + 2, gp0))
            gp0 += 2 * qb + 2
    NPAIR_TOT = gp0

    @blk.tensor
    def _(t):
        for Bi, (hb, qb, npair, g0) in enumerate(blocks):
            kvl = hb // 4
            obank = bank(4 + Bi % 2)
            for bp in range(npair):
                gp = g0 + bp
                sp = P[gp % 2]
                if Bi == 0 and bp == 0:
                    t.wait_ge(sV.h, V_NEGB)
                if gp >= 2:
                    t.wait_ge(sEX.h, gp - 1)
                last = None
                for j in range(2):
                    kc = 2 * bp + j
                    diag = kc >= 4 * qb
                    last = t.matmul(
                        sp[:, 512 * j:512 * (j + 1)],
                        kta[:, 4096 * kvl + 128 * kc:4096 * kvl + 128 * (kc + 1)],
                        bigA[:, 4096 * hb + 512 * qb:4096 * hb + 512 * (qb + 1)],
                        start=True, stop=not diag)
                    if diag:
                        d = kc - 4 * qb
                        last = t.matmul(sp[:, 512 * j:512 * (j + 1)], C["rampa"][:],
                                        C["rampb"][:, 512 * d:512 * (d + 1)],
                                        start=False, stop=True)
                sSC.inc(last)
                if Bi >= 1 and bp == 1:
                    t.wait_ge(sZ1.h, Bi)
                    if Bi >= 2:
                        t.wait_ge(sZ2.h, 2 * Bi - 3)
                    sTZ.inc(t.matmul(pzr[0:1, 0:512], C["onescol"][:], pbf[:],
                                     start=True, stop=True))
                if Bi >= 1 and bp == min(2, npair - 1):
                    t.wait_ge(sZ2.h, 2 * Bi)
                    if Bi >= 2:
                        t.wait_ge(sZ3.h, Bi - 1)
                    sTR.inc(t.matmul(pzr[:, 512:1024], C["onesrow"][:], rz_sb[:],
                                     start=True, stop=True))
                if bp >= 1:
                    gpa = gp - 1
                    bpa = bp - 1
                    t.wait_ge(sEX.h, gpa + 1)
                    if bpa == 0 and Bi >= 2:
                        t.wait_ge(sOE.h, Bi - 1)
                    lastav = None
                    for j in range(2):
                        kc = 2 * bpa + j
                        lastav = t.matmul(
                            obank,
                            vtm[:, 256 * kc + 128 * kvl:256 * kc + 128 * (kvl + 1)],
                            pT[:, 1024 * (gpa % 3) + 512 * j:
                                1024 * (gpa % 3) + 512 * (j + 1)],
                            start=(bpa == 0 and j == 0), stop=False)
                    sAV.inc(lastav)
            gpa = g0 + npair - 1
            t.wait_ge(sEX.h, gpa + 1)
            if npair - 1 == 0 and Bi >= 2:
                t.wait_ge(sOE.h, Bi - 1)
            lastav = None
            for j in range(2):
                kc = 2 * (npair - 1) + j
                lastav = t.matmul(
                    obank,
                    vtm[:, 256 * kc + 128 * kvl:256 * kc + 128 * (kvl + 1)],
                    pT[:, 1024 * (gpa % 3) + 512 * j:1024 * (gpa % 3) + 512 * (j + 1)],
                    start=(npair == 1 and j == 0), stop=(j == 1))
            sAV.inc(lastav)
        Bl = len(blocks)
        t.wait_ge(sZ1.h, Bl)
        t.wait_ge(sZ2.h, 2 * Bl - 3)
        sTZ.inc(t.matmul(pzr[0:1, 0:512], C["onescol"][:], pbf[:], start=True, stop=True))
        t.wait_ge(sZ2.h, 2 * Bl)
        t.wait_ge(sZ3.h, Bl - 1)
        sTR.inc(t.matmul(pzr[:, 512:1024], C["onesrow"][:], rz_sb[:], start=True, stop=True))

    @blk.scalar
    def _(sc):
        for Bi, (hb, qb, npair, g0) in enumerate(blocks):
            for bp in range(npair):
                gp = g0 + bp
                sc.wait_ge(sSC.h, gp + 1)
                if gp >= 3:
                    sc.wait_ge(sAV.h, gp - 2)
                    sc.wait_ge(sUA.h, gp - 2)
                sEX.inc(sc.activation(pT[:, 1024 * (gp % 3):1024 * (gp % 3 + 1)],
                                      P[gp % 2][:], AF.Exp, bias=negb[:, 0:1], scale=1.0))
                if Bi >= 1 and bp == 1:
                    sc.wait_ge(sTZ.h, Bi)
                    sZ2.inc(sc.activation(rz_sb[:], pzr[0:1, 0:512], (AF.Identity if NOLN else AF.Ln),
                                          bias=0.0, scale=1.0))
                    sc.wait_ge(sZ2.h, 2 * (Bi - 1) + 1)
                    sZ2.inc(sc.activation(rz_sb[:], rz_sb[:], AF.Exp, bias=0.0, scale=-1.0))
        sc.wait_ge(sTZ.h, len(blocks))
        sZ2.inc(sc.activation(rz_sb[:], pzr[0:1, 0:512], (AF.Identity if NOLN else AF.Ln),
                              bias=0.0, scale=1.0))
        sc.wait_ge(sZ2.h, 2 * len(blocks) - 1)
        sZ2.inc(sc.activation(rz_sb[:], rz_sb[:], AF.Exp, bias=0.0, scale=-1.0))

    @blk.vector
    def _(v):
        vq = []
        for Bi, (hb, qb, npair, g0) in enumerate(blocks):
            for bp in range(npair):
                gp = g0 + bp
                v.wait_ge(sEX.h, gp + 1)
                sV.inc(v.tensor_tensor(usb[:], pT[:, 1024 * (gp % 3):1024 * (gp % 3) + 512],
                                       pT[:, 1024 * (gp % 3) + 512:1024 * (gp % 3 + 1)],
                                       OP.add))
                v.wait_ge(sV.h, sV.n)
                if bp == 0:
                    sUA.inc(v.tensor_copy(psacc[:], usb[:]))
                else:
                    sUA.inc(v.tensor_tensor(psacc[:], psacc[:], usb[:], OP.add))
                if vq:
                    vq.pop(0)()
            while vq:
                vq.pop(0)()
            gend = g0 + npair

            def mk(Bi=Bi, hb=hb, qb=qb, gend=gend):
                obank = bank(4 + Bi % 2)

                def s_pbf():
                    v.wait_ge(sUA.h, gend)
                    sZ1.inc(v.tensor_copy(pbf[:], psacc[:]))

                def s_rzc():
                    v.wait_ge(sTR.h, Bi + 1)
                    sZ3.inc(v.tensor_copy(rzbc_sb[:], pzr[:, 512:1024]))

                def s_oev():
                    v.wait_ge(sZ3.h, Bi + 1)
                    v.wait_ge(sAV.h, gend)
                    if Bi >= 2:
                        v.wait_ge(sPO[Bi % 2].h, 16 * (Bi // 2))
                    sOE.inc(v.tensor_tensor(obuf[:, 512 * (Bi % 2):512 * (Bi % 2 + 1)],
                                            obank, rzbc_sb[:], OP.mult))
                return [s_pbf, s_rzc, s_oev]
            steps = mk()
            steps[0]()
            vq = steps[1:]
        while vq:
            vq.pop(0)()

    @blk.sync
    def _(s):
        for Bi, (hb, qb, npair, g0) in enumerate(blocks):
            s.wait_ge(sOE.h, Bi + 1)
            sPO[Bi % 2].dma(s.dma_start(
                out=o_in.ap()[128 * hb:128 * (hb + 1), 512 * qb:512 * (qb + 1)],
                in_=obuf[:, 512 * (Bi % 2):512 * (Bi % 2 + 1)]))

    @blk.gpsimd
    def _(g):
        g.wait_ge(sPO[0].h, 16 * 16)
        g.wait_ge(sPO[1].h, 16 * 16)
        sC.inc(g.collective_compute("AllGather", OP.bypass, replica_groups=GROUPS_HALF,
                                    ins=[o_in.ap()[0:512, :]], outs=[o_out_a.ap()]))
        g.wait_ge(sPO[0].h, sPO[0].n)
        g.wait_ge(sPO[1].h, sPO[1].n)
        sC.inc(g.collective_compute("AllGather", OP.bypass, replica_groups=GROUPS_HALF,
                                    ins=[o_in.ap()[512:1024, :]], outs=[o_out_b.ap()]))
    C_O = sC.n
    ph5b.close()
    if stop_phase == 5:
        @blk.gpsimd
        def _(g):
            g.wait_ge(sC.h, C_O)
        srcs = [(o_in.ap()[128 * i:128 * (i + 1), 0:4096], 128, 4096) for i in range(4)]
        ph5.close()
        return emit_stop_dump(srcs)
    ph5.close()

    ph6 = contextlib.ExitStack()
    phb6 = ph6.enter_context(nc.psum_tensor("phb6", [128, 512], BF16))
    rth = ph6.enter_context(nc.sbuf_tensor("rth", [128, 4 * TPC], BF16))
    xr = ph6.enter_context(nc.sbuf_tensor("xr", [128, 2 * 4096], F32))
    outb = ph6.enter_context(nc.sbuf_tensor("outb", [128, 2 * 512], F32))

    N_PAIRS = 8 * sum(2 * qb + 2 for qb in range(8))
    PERM_A = [8 * r + h for r in range(4) for h in range(4)]
    PERM_B = [8 * r + 4 + h for r in range(4) for h in range(4)]
    WO_PERM = PERM_A + PERM_B

    @blk.gpsimd
    def _(g):
        g.wait_ge(sC.h, C_O - 1)
        g.wait_ge(sSC.h, N_PAIRS)
        g.wait_ge(sAV.h, N_PAIRS)
        rankv2 = nc.values_load(rkt[:], engines=[mybir.EngineType.Pool],
                                min_val=0, max_val=7, skip_runtime_bounds_check=True)
        ooa = o_out_a.ap()
        for c in g.Switch(rankv2, 8):
            t0 = 1024 * (c % 4)
            for r in range(4):
                for h in range(4):
                    fc = 8 * r + h
                    g.dma_start(out=bigA[:, TPC * fc:TPC * (fc + 1)],
                                in_=ooa[512 * r + 128 * h:512 * r + 128 * (h + 1),
                                        t0:t0 + TPC]).then_inc(sG.h, 16)
    sG.n += 16 * 16
    G_OASS_A = sG.n

    @blk.gpsimd
    def _(g):
        g.wait_ge(sC.h, C_O)
        rankv3 = nc.values_load(rkt[:], engines=[mybir.EngineType.Pool],
                                min_val=0, max_val=7, skip_runtime_bounds_check=True)
        oob = o_out_b.ap()
        for c in g.Switch(rankv3, 8):
            t0 = 1024 * (c % 4)
            for r in range(4):
                for h in range(4):
                    fc = 8 * r + 4 + h
                    g.dma_start(out=bigA[:, TPC * fc:TPC * (fc + 1)],
                                in_=oob[512 * r + 128 * h:512 * r + 128 * (h + 1),
                                        t0:t0 + TPC]).then_inc(sG.h, 16)
    sG.n += 16 * 16
    G_OASS = sG.n

    def mk_wo_evac(ocg):
        def wo_evac(og, m, tb, ps, u_mark, t_mark):
            ocl = og * 2 + m
            oc = 4 * ocg + ocl

            @blk.vector
            def _(v, ocl=ocl, oc=oc, tb=tb, ps=ps, u_mark=u_mark):
                v.wait_ge(sU.h, u_mark)
                sE.inc(v.tensor_tensor(
                    rth[:, TPC * ocl + 512 * tb:TPC * ocl + 512 * (tb + 1)],
                    ps, bigB[:, TPC * oc + 512 * tb:TPC * oc + 512 * (tb + 1)], OP.mult))
        return wo_evac

    def mk_wo_gate(ocg):
        def gate(t):
            if ocg == 0:
                sOE.wait(t)
            sE.wait(t)
            sV.wait(t)
        return gate

    for ocg in range(8):
        @blk.sync
        def _(s, ocg=ocg):
            if ocg == 0:
                s.wait_ge(sSC.h, sSC.n)
                s.wait_ge(sAV.h, sAV.n)
                s.wait_ge(sUA.h, sUA.n)
                s.wait_ge(sOE.h, sOE.n)
                s.wait_ge(sV.h, sV.n)
            if ocg >= 2:
                s.wait_ge(sV.h, sV.n)
            for tc in range(8):
                sXs[0].dma(s.dma_start(
                    out=xr[:, 4096 * (ocg % 2) + 512 * tc:4096 * (ocg % 2) + 512 * (tc + 1)],
                    in_=ap["x"][128 * tc:128 * (tc + 1), 512 * ocg:512 * (ocg + 1)]))
        X_OCG = sXs[0].n

        def wo_igate(t, i, ocg=ocg):
            if i < 16:
                t.wait_ge(sG.h, G_OASS_A)
            else:
                t.wait_ge(sG.h, G_OASS)
        emit_proj(ap["wo"][2 * ocg:2 * ocg + 2], 2, mk_wo_evac(ocg),
                  first_gate=mk_wo_gate(ocg), serial4=True, iperm=WO_PERM,
                  i_gate=wo_igate)

        for tc in range(8):
            reg = phb6

            @blk.tensor
            def _(t, ocg=ocg, tc=tc, reg=reg):
                if tc == 0:
                    sE.wait(t)
                else:
                    sV.wait(t)
                last = None
                for j in range(4):
                    last = t.transpose(
                        reg[:, 128 * j:128 * (j + 1)],
                        rth[:, TPC * j + 128 * tc:TPC * j + 128 * (tc + 1)],
                        C["ident"][:])
                sT.inc(last)

            @blk.vector
            def _(v, ocg=ocg, tc=tc, reg=reg, X_OCG=X_OCG):
                sT.wait(v)
                v.wait_ge(sXs[0].h, X_OCG)
                gidx = 8 * ocg + tc
                if gidx >= 2:
                    v.wait_ge(sOs[gidx % 2].h, 16 * (gidx // 2))
                sV.inc(v.tensor_tensor(
                    outb[:, 512 * (tc % 2):512 * (tc % 2 + 1)], reg[:, 0:512],
                    xr[:, 4096 * (ocg % 2) + 512 * tc:4096 * (ocg % 2) + 512 * (tc + 1)],
                    OP.add))

            @blk.sync
            def _(s, ocg=ocg, tc=tc):
                sV.wait(s)
                gidx = 8 * ocg + tc
                sOs[gidx % 2].dma(s.dma_start(
                    out=y_out[128 * tc:128 * (tc + 1), 512 * ocg:512 * (ocg + 1)],
                    in_=outb[:, 512 * (tc % 2):512 * (tc % 2 + 1)]))

    @blk.sync
    def _(s):
        s.wait_ge(sOs[0].h, sOs[0].n)
        s.wait_ge(sOs[1].h, sOs[1].n)

    @blk.gpsimd
    def _(g):
        g.wait_ge(sOs[0].h, sOs[0].n)
        g.wait_ge(sOs[1].h, sOs[1].n)

    ph6.close()
    ctx.close()
    return nc



_CACHE = {}


def kernel(**inputs):
    in_maps = _prep_inputs(inputs)
    if "nc" not in _CACHE:
        _CACHE["nc"] = build_program(debug=False)
    nc = _CACHE["nc"]
    res = run_bass_kernel_spmd(nc, in_maps, core_ids=list(range(NCORE)))
    y = np.concatenate([res.results[c]["y"] for c in range(NCORE)], axis=0)
    return y.reshape(B, T, D).astype(np.float32)


# version 15
# speedup vs baseline: 1.0808x; 1.0808x over previous
import contextlib
import os
import numpy as np
NOLN = bool(os.environ.get('NOLN'))
import ml_dtypes
import concourse.bass as bass
import concourse.mybir as mybir
from concourse.alu_op_type import AluOpType as OP
from concourse.bass_utils import run_bass_kernel_spmd

F32 = mybir.dt.float32
BF16 = mybir.dt.bfloat16
I32 = mybir.dt.int32
AF = mybir.ActivationFunctionType
AX = mybir.AxisListType

B, T, D = 2, 4096, 4096
H, KV, HD = 32, 8, 128
NCORE = 8
TPC = 1024
E, R = 8, 16
NH = 32
CAPACITY = float(int(3.0 * (B * T) / E))
LN_EPS = 1e-5
RAMP_C = 60.0
NSLOT = 8

GROUPS_HALF = [[0, 1, 2, 3], [4, 5, 6, 7]]
GROUPS_ALL = [[0, 1, 2, 3, 4, 5, 6, 7]]



def _llama3_inv_freq():
    THETA, FACTOR, HI_FF, LO_FF, ORIG_MAX = 500000.0, 8.0, 4.0, 1.0, 8192
    inv = 1.0 / THETA ** (np.arange(0, HD, 2, dtype=np.float64) / HD)
    wavelen = 2.0 * np.pi / inv
    low_wl = ORIG_MAX / LO_FF
    high_wl = ORIG_MAX / HI_FF
    smooth = (ORIG_MAX / wavelen - LO_FF) / (HI_FF - LO_FF)
    scaled = np.where(wavelen > low_wl, inv / FACTOR, inv)
    mid = (wavelen <= low_wl) & (wavelen >= high_wl)
    scaled = np.where(mid, (1 - smooth) * inv / FACTOR + smooth * inv, scaled)
    return scaled.astype(np.float32)


def _rope_tables(pos0):
    inv = _llama3_inv_freq()
    pos = np.arange(pos0, pos0 + TPC, dtype=np.float32)
    fr = pos[None, :] * np.concatenate([inv, inv])[:, None]
    cos = np.cos(fr).astype(np.float32)
    sin = np.sin(fr).astype(np.float32)
    sf = sin.copy()
    sf[:64] = -sf[:64]
    return cos, sf


def _tile_w(w):
    I, O = w.shape
    t = w.reshape(I // 256, 2, 128, O // 256, 256)
    return np.ascontiguousarray(t.transpose(3, 0, 2, 1, 4).reshape(O // 256, I // 256, 128, 512))


def _ramp_consts():
    j = np.arange(128)[:, None]
    k = np.arange(128)[None, :]
    RA = ((j == 0) | ((k >= j) & (j >= 1))).astype(ml_dtypes.bfloat16)
    q = np.arange(512)[None, :]
    RBs = []
    for d in range(4):
        delta = 128 * d
        RB = np.where(j == 0, -RAMP_C * 128.0 * (q < delta),
                      np.where(j >= 1, -RAMP_C * (q < j + delta), 0.0))
        RBs.append(RB)
    return RA, np.stack(RBs).astype(ml_dtypes.bfloat16)


def _prep_inputs(inp):
    g = {}
    g["wq"] = _tile_w(np.asarray(inp["L0_attn_wq"]))
    g["wk"] = _tile_w(np.asarray(inp["L0_attn_wk"]))
    g["wv"] = _tile_w(np.asarray(inp["L0_attn_wv"]))
    g["wo"] = _tile_w(np.asarray(inp["L0_attn_wo"]))
    g["wt"] = _tile_w(np.ascontiguousarray(np.asarray(inp["L0_ffn_weight"]).T))
    A = np.asarray(inp["L0_ffn_lora_A"])
    Acat = np.ascontiguousarray(A.transpose(1, 0, 2).reshape(D, E * R))
    g["acat"] = np.ascontiguousarray(
        Acat.reshape(8, 4, 128, 128).transpose(0, 2, 1, 3).reshape(8, 128, 512))
    g["bcat"] = np.ascontiguousarray(np.asarray(inp["L0_ffn_lora_B"]).reshape(E * R, D))
    g["gw1t"] = np.ascontiguousarray(
        np.asarray(inp["L0_ffn_gw1"]).T.reshape(4, 8, 128, 32).transpose(0, 2, 1, 3)
        .reshape(4, 128, 256))
    g["gw2t"] = np.ascontiguousarray(np.asarray(inp["L0_ffn_gw2"]).T)
    g["gb1"] = np.ascontiguousarray(np.asarray(inp["L0_ffn_gb1"]).reshape(1, NH))
    g["lng"] = np.ascontiguousarray(np.asarray(inp["L0_ffn_ln_gamma"]).reshape(1, NH))
    g["lnb"] = np.ascontiguousarray(np.asarray(inp["L0_ffn_ln_beta"]).reshape(1, NH))
    g["gb2"] = np.ascontiguousarray(np.asarray(inp["L0_ffn_gb2"]).reshape(1, E))

    g["ident"] = np.eye(128, dtype=ml_dtypes.bfloat16)
    g["identf"] = np.eye(128, dtype=np.float32)
    RA, RB = _ramp_consts()
    g["rampa"] = RA
    g["rampb"] = RB
    e16 = np.zeros((E, E * R), dtype=ml_dtypes.bfloat16)
    for e in range(E):
        e16[e, e * R:(e + 1) * R] = ml_dtypes.bfloat16(1.0 / R)
    g["e16"] = e16
    g["onescol"] = np.ones((128, 1), dtype=ml_dtypes.bfloat16)
    g["onesrow"] = np.ones((1, 128), dtype=np.float32)
    g["negrow"] = np.full((1, 128), -1.0, dtype=ml_dtypes.bfloat16)

    x = np.asarray(inp["L0_attn_x"])
    scale = 1.0 / np.sqrt(HD)
    maps = []
    for c in range(NCORE):
        b, blkk = c // 4, c % 4
        pos0 = blkk * TPC
        cos, sf = _rope_tables(pos0)
        m = dict(g)
        m["x"] = np.ascontiguousarray(x[b, pos0:pos0 + TPC])
        m["cosq"] = np.ascontiguousarray(cos * scale)
        m["sfq"] = np.ascontiguousarray(sf * scale)
        m["cosk"] = cos
        m["sfk"] = sf
        m["rank"] = np.array([[c]], dtype=np.int32)
        maps.append(m)
    return maps



class Sem:
    def __init__(self, ctx, nc, name):
        self.h = ctx.enter_context(nc.semaphore(name))
        self.n = 0

    def inc(self, instr, k=1):
        instr.then_inc(self.h, k)
        self.n += k
        return self.n

    def dma(self, instr):
        return self.inc(instr, 16)

    def wait(self, eng, val=None):
        v = self.n if val is None else val
        if v > 0:
            eng.wait_ge(self.h, v)


def build_program(debug=False, stop_phase=None, no_xpose=False, xp_simple=False,
                  ktiles=8):
    nc = bass.Bass("TRN2", target_bir_lowering=False)
    ctx = contextlib.ExitStack()

    class LazyAP(dict):
        def __init__(self, specs):
            super().__init__()
            self.specs = dict(specs)

        def __missing__(self, name):
            sh, dt = self.specs[name]
            v = nc.dram_tensor(name, sh, dt, kind="ExternalInput").ap()
            self[name] = v
            return v

    _specs = {}
    for name, shape, dt in [
        ("x", [TPC, D], F32),
        ("wq", [16, 16, 128, 512], F32), ("wk", [4, 16, 128, 512], F32),
        ("wv", [4, 16, 128, 512], F32), ("wo", [16, 16, 128, 512], F32),
        ("wt", [16, 16, 128, 512], F32),
        ("acat", [8, 128, 512], F32), ("bcat", [128, D], F32),
        ("gw1t", [4, 128, 256], F32), ("gw2t", [32, 8], F32),
        ("gb1", [1, NH], F32), ("lng", [1, NH], F32), ("lnb", [1, NH], F32),
        ("gb2", [1, E], F32),
        ("cosq", [128, TPC], F32), ("sfq", [128, TPC], F32),
        ("cosk", [128, TPC], F32), ("sfk", [128, TPC], F32),
        ("ident", [128, 128], BF16), ("identf", [128, 128], F32),
        ("rampa", [128, 128], BF16),
        ("rampb", [4, 128, 512], BF16), ("e16", [E, E * R], BF16),
        ("onescol", [128, 1], BF16), ("onesrow", [1, 128], F32),
        ("negrow", [1, 128], BF16), ("rank", [1, 1], I32),
    ]:
        _specs[name] = (shape, dt)
    ap = LazyAP(_specs)
    y_out = nc.dram_tensor("y", [TPC, D], F32, kind="ExternalOutput").ap()
    dbg = nc.dram_tensor("dbg", [128, 256], F32, kind="ExternalOutput").ap() if debug else None

    qkv_in = nc.dram_tensor("qkv_in", [6144, TPC], BF16)
    qkv_out = nc.dram_tensor("qkv_out", [4 * 6144, TPC], BF16)
    o_in = nc.dram_tensor("o_in", [TPC, T], BF16)
    o_out_a = nc.dram_tensor("o_out_a", [4 * 512, T], BF16)
    o_out_b = nc.dram_tensor("o_out_b", [4 * 512, T], BF16)
    cnt_in = nc.dram_tensor("cnt_in", [1, 2 * E], F32)
    cnt_out = nc.dram_tensor("cnt_out", [1, 2 * E], F32)

    bigA = ctx.enter_context(nc.sbuf_tensor("bigA", [128, 32 * TPC], BF16))
    bigB = ctx.enter_context(nc.sbuf_tensor("bigB", [128, 32 * TPC], BF16))
    C = {}
    for n, sh, dt in [("ident", [128, 128], BF16), ("identf", [128, 128], F32),
                      ("rampa", [128, 128], BF16),
                      ("rampb", [128, 4 * 512], BF16), ("e16", [E, E * R], BF16),
                      ("onescol", [128, 1], BF16), ("onesrow", [1, 128], F32),
                      ("negrow", [1, 128], BF16), ("gw2t", [32, 8], BF16),
                      ("gw2tf", [32, 8], F32),
                      ("gb1", [1, NH], F32), ("lng", [1, NH], F32), ("lnb", [1, NH], F32),
                      ("gb2", [1, E], F32)]:
        C[n] = ctx.enter_context(nc.sbuf_tensor("c_" + n, sh, dt))
    wslot = ctx.enter_context(nc.sbuf_tensor("wslot", [128, NSLOT * 256], BF16))
    gbc = ctx.enter_context(nc.sbuf_tensor("gbc", [128, 3 * NH + E], F32))
    small = ctx.enter_context(nc.sbuf_tensor("small", [128, 32], F32))
    rkt = ctx.enter_context(nc.sbuf_tensor("rkt", [1, 1], I32))
    cpad = ctx.enter_context(nc.sbuf_tensor("cpad", [128, 2], F32))
    ucol = ctx.enter_context(nc.sbuf_tensor("ucol", [128, 16 * 8], F32))
    abc = ctx.enter_context(nc.sbuf_tensor("abc", [128, 2 * E], F32))
    negb = ctx.enter_context(nc.sbuf_tensor("negb", [128, 1], F32))
    phE = contextlib.ExitStack()
    scrA = phE.enter_context(nc.sbuf_tensor("scrA", [128, 1024], F32))
    scrB = phE.enter_context(nc.sbuf_tensor("scrB", [128, 1024], F32))
    pub = phE.enter_context(nc.sbuf_tensor("pub", [128, 2 * TPC], BF16))

    P = [ctx.enter_context(nc.psum_tensor(f"P{i}", [128, 1024], F32)) for i in range(3)]

    def bank(i):
        return P[i // 2][:, 512 * (i % 2):512 * (i % 2 + 1)]

    sW = Sem(ctx, nc, "sW")
    sWG = Sem(ctx, nc, "sWG")
    sWR = Sem(ctx, nc, "sWR")
    sWA = Sem(ctx, nc, "sWA")
    sWC = Sem(ctx, nc, "sWC")
    sWT = [Sem(ctx, nc, f"sWT{i}") for i in range(4)]
    sXs = [Sem(ctx, nc, "sX0"), Sem(ctx, nc, "sX1")]
    sP = Sem(ctx, nc, "sP")
    sPB = [Sem(ctx, nc, "sPB0"), Sem(ctx, nc, "sPB1")]
    sPO = [Sem(ctx, nc, "sPO0"), Sem(ctx, nc, "sPO1")]
    sC = Sem(ctx, nc, "sC")
    sT = Sem(ctx, nc, "sT")
    sU = Sem(ctx, nc, "sU")
    sE = Sem(ctx, nc, "sE")
    sV = Sem(ctx, nc, "sV")
    sS = Sem(ctx, nc, "sS")
    sSC = Sem(ctx, nc, "sSC")
    sEX = Sem(ctx, nc, "sEX")
    sAV = Sem(ctx, nc, "sAV")
    sUA = Sem(ctx, nc, "sUA")
    sOE = Sem(ctx, nc, "sOE")
    sG = Sem(ctx, nc, "sG")
    sOs = [Sem(ctx, nc, "sO0"), Sem(ctx, nc, "sO1")]

    blk = ctx.enter_context(nc.Block())

    @blk.gpsimd
    def _(g):
        g.memset(cpad[:, 0:1], LN_EPS)
        nc.const_aps.register(F32, LN_EPS, cpad[:, 0:1])
        sW.dma(g.dma_start(out=rkt[:], in_=ap["rank"]))
        for n in ["ident", "identf", "rampa", "e16", "onescol", "onesrow",
                  "negrow", "gb1", "lng", "lnb", "gb2"]:
            sW.dma(g.dma_start(out=C[n][:], in_=ap[n]))
        sW.dma(g.dma_start(out=C["gw2tf"][:], in_=ap["gw2t"]))
        for d in range(4):
            sW.dma(g.dma_start(out=C["rampb"][:, 512 * d:512 * (d + 1)], in_=ap["rampb"][d]))
    W_CONST = sW.n

    @blk.tensor
    def _(t):
        t.wait_ge(sW.h, W_CONST)
        t.matmul(P[0][:, 0:NH], C["onesrow"][:], C["gb1"][:], start=True, stop=True)
        t.matmul(P[0][:, NH:2 * NH], C["onesrow"][:], C["lng"][:], start=True, stop=True)
        t.matmul(P[0][:, 2 * NH:3 * NH], C["onesrow"][:], C["lnb"][:], start=True, stop=True)
        sT.inc(t.matmul(P[0][:, 3 * NH:3 * NH + E], C["onesrow"][:], C["gb2"][:],
                        start=True, stop=True))

    @blk.vector
    def _(v):
        sT.wait(v)
        v.tensor_copy(C["gw2t"][:], C["gw2tf"][:])
        sV.inc(v.tensor_copy(gbc[:, 0:3 * NH + E], P[0][:, 0:3 * NH + E]))

    ph1 = contextlib.ExitStack()
    if stop_phase == 0:
        def emit_stop0():
            sD = Sem(ctx, nc, "sD")

            @blk.gpsimd
            def _(g):
                for s in [sW, sT, sV]:
                    s.wait(g)
                sD.dma(g.dma_start(out=y_out[0:128, 0:104], in_=gbc[:, 0:104]))
                g.wait_ge(sD.h, sD.n)
            phE.close()
            ctx.close()
            return nc
        return emit_stop0()
    xbuf = ph1.enter_context(nc.sbuf_tensor("xbuf", [128, 2 * D], F32))
    xn_tm = ph1.enter_context(nc.sbuf_tensor("xn_tm", [128, 2 * D], BF16))
    phb1a = ph1.enter_context(nc.psum_tensor("phb1a", [128, 512], BF16))
    phb1b = ph1.enter_context(nc.psum_tensor("phb1b", [128, 512], BF16))
    st = small

    @blk.sync
    def _(s):
        for i in range(2):
            sXs[i].dma(s.dma_start(out=xbuf[:, D * i:D * (i + 1)],
                                   in_=ap["x"][128 * i:128 * (i + 1), :]))

    LN_S0 = sS.n
    LN_T0 = sT.n
    for i in range(ktiles):
        sl = D * (i % 2)
        sb = 16 * (i % 2)

        @blk.vector
        def _(v, i=i, sl=sl, sb=sb):
            v.wait_ge(sXs[i % 2].h, 16 * (i // 2 + 1))
            if i >= 2:
                v.wait_ge(sS.h, LN_S0 + 3 * (i - 1))
            def ve(instr):
                sV.inc(instr)
                v.wait_ge(sV.h, sV.n)
            v.tensor_reduce(st[:, sb:sb + 1], xbuf[:, sl:sl + D], AX.X, OP.add)
            v.tensor_tensor(scrA[:, 0:1024], xbuf[:, sl:sl + 1024],
                            xbuf[:, sl:sl + 1024], OP.mult)
            ve(v.tensor_tensor(scrB[:, 0:1024], xbuf[:, sl + 1024:sl + 2048],
                               xbuf[:, sl + 1024:sl + 2048], OP.mult))
            v.tensor_reduce(st[:, sb + 1:sb + 2], scrA[:, 0:1024], AX.X, OP.add)
            ve(v.tensor_reduce(st[:, sb + 6:sb + 7], scrB[:, 0:1024], AX.X, OP.add))
            v.tensor_tensor(scrA[:, 0:1024], xbuf[:, sl + 2048:sl + 3072],
                            xbuf[:, sl + 2048:sl + 3072], OP.mult)
            ve(v.tensor_tensor(scrB[:, 0:1024], xbuf[:, sl + 3072:sl + 4096],
                               xbuf[:, sl + 3072:sl + 4096], OP.mult))
            ve(v.tensor_tensor(st[:, sb + 1:sb + 2], st[:, sb + 1:sb + 2],
                               st[:, sb + 6:sb + 7], OP.add))
            v.tensor_reduce(st[:, sb + 6:sb + 7], scrA[:, 0:1024], AX.X, OP.add)
            ve(v.tensor_reduce(st[:, sb + 7:sb + 8], scrB[:, 0:1024], AX.X, OP.add))
            ve(v.tensor_tensor(st[:, sb + 6:sb + 7], st[:, sb + 6:sb + 7],
                               st[:, sb + 7:sb + 8], OP.add))
            ve(v.tensor_tensor(st[:, sb + 1:sb + 2], st[:, sb + 1:sb + 2],
                               st[:, sb + 6:sb + 7], OP.add))
            ve(v.tensor_scalar(out=st[:, sb + 2:sb + 3], in0=st[:, sb:sb + 1],
                               scalar1=1.0 / D, scalar2=None, op0=OP.mult))
            ve(v.tensor_tensor(st[:, sb + 3:sb + 4], st[:, sb + 2:sb + 3],
                               st[:, sb + 2:sb + 3], OP.mult))
            ve(v.tensor_scalar(out=st[:, sb + 3:sb + 4], in0=st[:, sb + 1:sb + 2],
                               scalar1=1.0 / D, scalar2=st[:, sb + 3:sb + 4],
                               op0=OP.mult, op1=OP.subtract))

        @blk.scalar
        def _(sc, sb=sb):
            sV.wait(sc)
            sS.inc(sc.activation(st[:, sb + 4:sb + 5], st[:, sb + 3:sb + 4],
                                 (AF.Identity if NOLN else AF.Ln), bias=LN_EPS, scale=1.0))
            sc.wait_ge(sS.h, sS.n)
            sS.inc(sc.activation(st[:, sb + 4:sb + 5], st[:, sb + 4:sb + 5],
                                 AF.Exp, bias=0.0, scale=-0.5))

        @blk.vector
        def _(v, sb=sb):
            sS.wait(v)
            sV.inc(v.tensor_tensor(st[:, sb + 5:sb + 6], st[:, sb + 2:sb + 3],
                                   st[:, sb + 4:sb + 5], OP.mult))
            v.wait_ge(sV.h, sV.n)
            sV.inc(v.tensor_scalar(out=st[:, sb + 5:sb + 6], in0=st[:, sb + 5:sb + 6],
                                   scalar1=-1.0, scalar2=None, op0=OP.mult))

        @blk.scalar
        def _(sc, i=i, sl=sl, sb=sb):
            sV.wait(sc)
            if i >= 2 and not no_xpose:
                sc.wait_ge(sT.h, LN_T0 + 8 * (i - 1))
            sS.inc(sc.activation(xn_tm[:, sl:sl + D], xbuf[:, sl:sl + D], AF.Identity,
                                 bias=st[:, sb + 5:sb + 6], scale=st[:, sb + 4:sb + 5]))

        @blk.sync
        def _(s, i=i, sl=sl):
            if i + 2 < ktiles:
                sS.wait(s)
                sXs[i % 2].dma(s.dma_start(out=xbuf[:, sl:sl + D],
                                           in_=ap["x"][128 * (i + 2):128 * (i + 3), :]))

        for fg in range(8 if not no_xpose else 0):
            reg = (phb1a if fg % 2 == 0 else phb1b)[:, 0:512]

            @blk.tensor
            def _(t, i=i, sl=sl, fg=fg, reg=reg):
                if fg == 0:
                    t.wait_ge(sS.h, LN_S0 + 3 * (i + 1))
                if fg >= 2:
                    t.wait_ge(sV.h, sV.n)
                last = None
                for j in range(4):
                    fc = 4 * fg + j
                    last = t.transpose(reg[:, 128 * j:128 * (j + 1)],
                                       xn_tm[:, sl + 128 * fc:sl + 128 * (fc + 1)],
                                       C["ident"][:])
                sT.inc(last)

            @blk.vector
            def _(v, i=i, fg=fg, reg=reg):
                sT.wait(v)
                out = bigA[:].rearrange("p (f c) -> p f c", c=TPC)[
                    :, 4 * fg:4 * fg + 4, 128 * i:128 * (i + 1)]
                sV.inc(v.tensor_copy(out, reg[:].rearrange("p (f c) -> p f c", c=128)))

    V_XN = sV.n

    def emit_stop_dump(srcs, stacks=()):
        sD = Sem(ctx, nc, "sD")

        @blk.gpsimd
        def _(g):
            for s in [sW, sWG, sWR, sWA, sWC, sP, sC, sT, sU, sE, sV, sS,
                      sSC, sEX, sAV, sUA, sOE, sG] + sWT + sXs + sPB + sPO + sOs:
                s.wait(g)
            for i, (src_ap, nr, ncol) in enumerate(srcs):
                sD.dma(g.dma_start(out=y_out[128 * i:128 * i + nr, 0:ncol], in_=src_ap))
            g.wait_ge(sD.h, sD.n)
        for stk in stacks:
            stk.close()
        ctx.close()
        return nc

    if stop_phase != 1:
        ph1.close()
    if stop_phase == 1:
        if no_xpose:
            return emit_stop_dump([(xn_tm[:, 0:D], 128, D), (xn_tm[:, D:2 * D], 128, D)],
                                  stacks=(ph1, phE))
        return emit_stop_dump([(bigA[:, TPC * i:TPC * (i + 1)], 128, TPC) for i in range(8)],
                              stacks=(ph1, phE))

    ph2 = contextlib.ExitStack()
    gw1t_sb = ph2.enter_context(nc.sbuf_tensor("gw1t_sb", [128, 32 * 32], BF16))
    htv = ph2.enter_context(nc.sbuf_tensor("htv", [32, TPC], BF16))
    hs = ph2.enter_context(nc.sbuf_tensor("hs", [128, 2 * NH], F32))
    g0 = ph2.enter_context(nc.sbuf_tensor("g0", [128, E], F32))
    mbuf = ph2.enter_context(nc.sbuf_tensor("mbuf", [128, 2 * E], BF16))
    cnt_sb = ph2.enter_context(nc.sbuf_tensor("cnt_sb", [8, 2], F32))

    @blk.gpsimd
    def _(g):
        sV.wait(g)
        sS.wait(g)
        sT.wait(g)
        for i in range(4):
            sWG.dma(g.dma_start(out=gw1t_sb[:, 256 * i:256 * (i + 1)], in_=ap["gw1t"][i]))
    W_GW1 = sWG.n

    for tc in range(8):
        @blk.tensor
        def _(t, tc=tc):
            if tc == 0:
                t.wait_ge(sWG.h, W_GW1)
                t.wait_ge(sV.h, V_XN)
            else:
                sV.wait(t)
            last = None
            for i in range(32):
                last = t.matmul(P[0][:, 0:NH],
                                bigA[:, TPC * i + 128 * tc:TPC * i + 128 * (tc + 1)],
                                gw1t_sb[:, NH * i:NH * (i + 1)],
                                start=(i == 0), stop=(i == 31))
            sT.inc(last)

        @blk.vector
        def _(v, tc=tc):
            sT.wait(v)
            sS.wait(v)
            sV.inc(v.tensor_tensor(hs[:, 0:NH], P[0][:, 0:NH], gbc[:, 0:NH], OP.add))
            v.wait_ge(sV.h, sV.n)
            v.tensor_reduce(st[:, 8:9], hs[:, 0:NH], AX.X, OP.add)
            sV.inc(v.tensor_tensor(hs[:, NH:2 * NH], hs[:, 0:NH], hs[:, 0:NH], OP.mult))
            v.wait_ge(sV.h, sV.n)
            sV.inc(v.tensor_reduce(st[:, 9:10], hs[:, NH:2 * NH], AX.X, OP.add))
            v.wait_ge(sV.h, sV.n)
            sV.inc(v.tensor_scalar(out=st[:, 10:11], in0=st[:, 8:9], scalar1=1.0 / NH,
                                   scalar2=None, op0=OP.mult))
            v.wait_ge(sV.h, sV.n)
            sV.inc(v.tensor_tensor(st[:, 11:12], st[:, 10:11], st[:, 10:11], OP.mult))
            v.wait_ge(sV.h, sV.n)
            sV.inc(v.tensor_scalar(out=st[:, 11:12], in0=st[:, 9:10], scalar1=1.0 / NH,
                                   scalar2=st[:, 11:12], op0=OP.mult, op1=OP.subtract))

        @blk.scalar
        def _(sc):
            sV.wait(sc)
            sS.inc(sc.activation(st[:, 12:13], st[:, 11:12], (AF.Identity if NOLN else AF.Ln), bias=LN_EPS, scale=1.0))
            sc.wait_ge(sS.h, sS.n)
            sS.inc(sc.activation(st[:, 12:13], st[:, 12:13], AF.Exp, bias=0.0, scale=-0.5))

        @blk.vector
        def _(v, tc=tc):
            sS.wait(v)
            sV.inc(v.tensor_scalar(out=hs[:, 0:NH], in0=hs[:, 0:NH], scalar1=st[:, 10:11],
                                   scalar2=st[:, 12:13], op0=OP.subtract, op1=OP.mult))
            v.wait_ge(sV.h, sV.n)
            sV.inc(v.tensor_tensor(hs[:, 0:NH], hs[:, 0:NH], gbc[:, NH:2 * NH], OP.mult))
            v.wait_ge(sV.h, sV.n)
            sV.inc(v.tensor_tensor(hs[:, 0:NH], hs[:, 0:NH], gbc[:, 2 * NH:3 * NH], OP.add))
            v.wait_ge(sV.h, sV.n)
            sV.inc(v.tensor_scalar(out=hs[:, NH:2 * NH], in0=hs[:, 0:NH], scalar1=0.0,
                                   scalar2=None, op0=OP.max))

        @blk.tensor
        def _(t, tc=tc):
            sV.wait(t)
            sT.inc(t.transpose(P[1][0:NH, 0:128], hs[:, NH:2 * NH], C["identf"][:]))

        @blk.vector
        def _(v, tc=tc):
            sT.wait(v)
            sV.inc(v.tensor_copy(htv[:, 128 * tc:128 * (tc + 1)], P[1][0:NH, 0:128]))

        @blk.tensor
        def _(t, tc=tc):
            sV.wait(t)
            sT.inc(t.matmul(P[1][:, 512:512 + E], htv[:, 128 * tc:128 * (tc + 1)],
                            C["gw2t"][:], start=True, stop=True))

        @blk.vector
        def _(v, tc=tc):
            sT.wait(v)
            sV.inc(v.tensor_tensor(g0[:, 0:E], P[1][:, 512:512 + E],
                                   gbc[:, 3 * NH:3 * NH + E], OP.add))
            v.wait_ge(sV.h, sV.n)
            sV.inc(v.tensor_reduce(st[:, 16:17], g0[:, 0:E], AX.X, OP.max))
            v.wait_ge(sV.h, sV.n)
            sV.inc(v.tensor_scalar(out=mbuf[:, 0:E], in0=g0[:, 0:E], scalar1=st[:, 16:17],
                                   scalar2=None, op0=OP.is_equal))
            v.wait_ge(sV.h, sV.n)
            sV.inc(v.scalar_tensor_tensor(out=hs[:, 0:E], in0=mbuf[:, 0:E], scalar=-1e9,
                                          in1=g0[:, 0:E], op0=OP.mult, op1=OP.add))
            v.wait_ge(sV.h, sV.n)
            sV.inc(v.tensor_reduce(st[:, 17:18], hs[:, 0:E], AX.X, OP.max))
            v.wait_ge(sV.h, sV.n)
            sV.inc(v.tensor_scalar(out=mbuf[:, E:2 * E], in0=hs[:, 0:E], scalar1=st[:, 17:18],
                                   scalar2=None, op0=OP.is_equal))
            sV.inc(v.tensor_scalar(out=st[:, 18:19], in0=st[:, 16:17], scalar1=-1.0,
                                   scalar2=None, op0=OP.mult))
            sV.inc(v.tensor_scalar(out=st[:, 19:20], in0=st[:, 17:18], scalar1=-1.0,
                                   scalar2=None, op0=OP.mult))

        @blk.scalar
        def _(sc):
            sV.wait(sc)
            sS.inc(sc.activation(st[:, 20:21], st[:, 16:17], AF.Sigmoid,
                                 bias=st[:, 19:20], scale=1.0))
            sS.inc(sc.activation(st[:, 21:22], st[:, 17:18], AF.Sigmoid,
                                 bias=st[:, 18:19], scale=1.0))

        @blk.vector
        def _(v, tc=tc):
            sS.wait(v)
            sV.inc(v.tensor_scalar(out=ucol[:, 16 * tc:16 * tc + E], in0=mbuf[:, 0:E],
                                   scalar1=st[:, 20:21], scalar2=None, op0=OP.mult))
            sV.inc(v.tensor_scalar(out=ucol[:, 16 * tc + E:16 * tc + 2 * E],
                                   in0=mbuf[:, E:2 * E],
                                   scalar1=st[:, 21:22], scalar2=None, op0=OP.mult))

        @blk.tensor
        def _(t, tc=tc):
            sV.wait(t)
            t.matmul(P[2][0:E, 0:1], mbuf[:, 0:E], C["onescol"][:],
                     start=(tc == 0), stop=(tc == 7))
            last = t.matmul(P[2][0:E, 512:513], mbuf[:, E:2 * E], C["onescol"][:],
                            start=(tc == 0), stop=(tc == 7))
            if tc == 7:
                sT.inc(last)

    @blk.vector
    def _(v):
        sT.wait(v)
        v.tensor_copy(cnt_sb[:, 0:1], P[2][0:E, 0:1])
        sV.inc(v.tensor_copy(cnt_sb[:, 1:2], P[2][0:E, 512:513]))

    @blk.sync
    def _(s):
        sV.wait(s)
        sP.dma(s.dma_start(out=cnt_in.ap()[0:1, 0:E], in_=cnt_sb[0:E, 0:1]))
        sP.dma(s.dma_start(out=cnt_in.ap()[0:1, E:2 * E], in_=cnt_sb[0:E, 1:2]))

    @blk.gpsimd
    def _(g):
        sP.wait(g)
        sC.inc(g.collective_compute("AllReduce", OP.add, replica_groups=GROUPS_ALL,
                                    ins=[cnt_in.ap()], outs=[cnt_out.ap()]))
    C_CNT = sC.n
    if stop_phase == 2:
        srcs = [(ucol[:, 0:128], 128, 128), (gbc[:, 0:104], 128, 104),
                (cnt_out.ap(), 1, 16), (hs[:, 0:2 * NH], 128, 2 * NH),
                (htv[0:32, 0:TPC], 32, TPC), (g0[:, 0:E], 128, E),
                (gw1t_sb[:, 0:1024], 128, 1024)]
        ph2.close()
        return emit_stop_dump(srcs, stacks=(phE,))
    ph2.close()

    state = {"gi": 0}

    def emit_proj(wdram, n_og, evac_fn, inject_fn=None, first_gate=None,
                  bank_fn=None, serial4=False, iperm=None, i_gate=None):
        bk = bank_fn if bank_fn is not None else bank
        e_base = sE.n
        gi0 = state["gi"]
        for og in range(n_og):
            @blk.gpsimd
            def _(g, og=og):
                for ip in range(16):
                    gp_ = (gi0 // 2) + og * 16 + ip
                    icp = (2 * ip) if iperm is None else iperm[2 * ip]
                    if gp_ >= 4:
                        g.wait_ge(sU.h, 2 * (gp_ - 3))
                    sWT[gp_ % 4].dma(g.dma_start(
                        out=wslot[:, 512 * (gp_ % 4):512 * (gp_ % 4 + 1)],
                        in_=wdram[og, icp // 2]))

            @blk.tensor
            def _(t, og=og):
                if og == 0 and first_gate is not None:
                    first_gate(t)
                if serial4:
                    if e_base + 4 * og > 0:
                        t.wait_ge(sE.h, e_base + 4 * og)
                    b0 = 0
                else:
                    if e_base + 4 * (og - 1) > 0:
                        t.wait_ge(sE.h, e_base + 4 * (og - 1))
                    b0 = 4 * (og % 2)
                for i in range(32):
                    gi = gi0 + og * 32 + i
                    ic = i if iperm is None else iperm[i]
                    gp_ = gi // 2
                    t.wait_ge(sWT[gp_ % 4].h, 16 * (gp_ // 4 + 1))
                    if i_gate is not None:
                        i_gate(t, i)
                    off = 512 * (gp_ % 4) + 256 * (gi % 2)
                    last = None
                    for m in range(2):
                        for tb in range(2):
                            last = t.matmul(
                                bk(b0 + 2 * m + tb),
                                wslot[:, off + 128 * m:off + 128 * (m + 1)],
                                bigA[:, TPC * ic + 512 * tb:TPC * ic + 512 * (tb + 1)],
                                start=(i == 0),
                                stop=(i == 31 and inject_fn is None))
                    sU.inc(last)
                if inject_fn is not None:
                    last = None
                    for m in range(2):
                        for tb in range(2):
                            lh, rh = inject_fn(og, m, tb)
                            last = t.matmul(bk(b0 + 2 * m + tb), lh, rh,
                                            start=False, stop=True)
                    sT.inc(last)
            u_mark = sU.n
            t_mark = sT.n
            b0e = 0 if serial4 else 4 * (og % 2)
            for m in range(2):
                for tb in range(2):
                    evac_fn(og, m, tb, bk(b0e + 2 * m + tb), u_mark, t_mark)
        state["gi"] = gi0 + n_og * 32

    phf = contextlib.ExitStack()
    PHF = phf.enter_context(nc.psum_tensor("PHF", [128, 1024], F32))

    def bank8(i):
        return bank(i) if i < 6 else PHF[:, 512 * (i - 6):512 * (i - 5)]

    ph3 = contextlib.ExitStack()
    rope = {}
    for n in ["cosq", "sfq", "cosk", "sfk"]:
        rope[n] = ph3.enter_context(nc.sbuf_tensor(n + "_sb", [128, TPC], F32))

    @blk.gpsimd
    def _(g):
        sV.wait(g)
        sS.wait(g)
        sT.wait(g)
        for n in ["cosq", "sfq", "cosk", "sfk"]:
            sWR.dma(g.dma_start(out=rope[n][:], in_=ap[n]))
    W_ROPE = sWR.n

    pub_state = {"n": 0, "slot": [0, 0]}

    def mk_qkv_evac(cosn, sfn, row0, _unused):
        def evac(og, m, tb, ps, u_mark, t_mark):
            mg = og * 2 + m
            slot = mg % 2

            @blk.vector
            def _(v, og=og, m=m, tb=tb, ps=ps, u_mark=u_mark, mg=mg, slot=slot):
                v.wait_ge(sU.h, u_mark)
                if tb == 0 and pub_state["slot"][slot] > 0:
                    v.wait_ge(sPB[slot].h, 16 * pub_state["slot"][slot])
                dst = pub[:, TPC * slot + 512 * tb:TPC * slot + 512 * (tb + 1)]
                if cosn is None:
                    sE.inc(v.tensor_copy(dst, ps))
                else:
                    cs = rope[cosn]
                    sf = rope[sfn]
                    v.wait_ge(sE.h, sE.n)
                    v.tensor_tensor(scrA[:, 0:512], ps, cs[:, 512 * tb:512 * (tb + 1)], OP.mult)
                    v.tensor_tensor(scrB[0:64, 0:512], ps[64:128, :],
                                    sf[0:64, 512 * tb:512 * (tb + 1)], OP.mult)
                    sV.inc(v.tensor_tensor(scrB[64:128, 0:512], ps[0:64, :],
                                           sf[64:128, 512 * tb:512 * (tb + 1)], OP.mult))
                    v.wait_ge(sV.h, sV.n)
                    sE.inc(v.tensor_tensor(dst, scrA[:, 0:512], scrB[:, 0:512], OP.add))

            if tb == 1:
                @blk.sync
                def _(s, mg=mg, slot=slot):
                    s.wait_ge(sE.h, sE.n)
                    sPB[slot].dma(s.dma_start(
                        out=qkv_in.ap()[row0 + 128 * mg:row0 + 128 * (mg + 1), :],
                        in_=pub[:, TPC * slot:TPC * (slot + 1)]))
                pub_state["slot"][slot] += 1
                pub_state["n"] += 1
        return evac

    def gate_rope(t):
        t.wait_ge(sWR.h, W_ROPE)
        sV.wait(t)

    emit_proj(ap["wk"], 4, mk_qkv_evac("cosk", "sfk", 4096, None), first_gate=gate_rope,
              bank_fn=bank8)
    emit_proj(ap["wv"], 4, mk_qkv_evac(None, None, 5120, None), bank_fn=bank8)
    emit_proj(ap["wq"], 16, mk_qkv_evac("cosq", "sfq", 0, None), bank_fn=bank8)

    @blk.gpsimd
    def _(g):
        for slot in range(2):
            g.wait_ge(sPB[slot].h, 16 * pub_state["slot"][slot])
        sC.inc(g.collective_compute("AllGather", OP.bypass, replica_groups=GROUPS_HALF,
                                    ins=[qkv_in.ap()], outs=[qkv_out.ap()]))
    C_QKV = sC.n
    if stop_phase == 3:
        @blk.gpsimd
        def _(g):
            g.wait_ge(sC.h, C_QKV)
        srcs = [(qkv_out.ap()[6144 * 0 + 128 * 0:6144 * 0 + 128, 0:1024], 128, 1024),
                (qkv_out.ap()[6144 * 1 + 4096:6144 * 1 + 4224, 0:1024], 128, 1024),
                (qkv_out.ap()[6144 * 2 + 5120:6144 * 2 + 5248, 0:1024], 128, 1024),
                (qkv_out.ap()[6144 * 3 + 2048:6144 * 3 + 2176, 0:1024], 128, 1024)]
        ph3.close()
        return emit_stop_dump(srcs, stacks=(phf, phE))
    ph3.close()

    ph4 = contextlib.ExitStack()
    acat_sb = ph4.enter_context(nc.sbuf_tensor("acat_sb", [128, 8 * 512], BF16))
    bcat_sb = ph4.enter_context(nc.sbuf_tensor("bcat_sb", [128, D], BF16))
    tmp_sb = ph4.enter_context(nc.sbuf_tensor("tmp_sb", [128, TPC], BF16))
    tmps_sb = ph4.enter_context(nc.sbuf_tensor("tmps_sb", [128, TPC], BF16))
    alw = ph4.enter_context(nc.sbuf_tensor("alw", [1, 2 * E], F32))
    wg8 = ph4.enter_context(nc.sbuf_tensor("wg8", [128, 8 * 8], F32))
    wgT = ph4.enter_context(nc.sbuf_tensor("wgT", [8, TPC], BF16))

    @blk.gpsimd
    def _(g):
        sE.wait(g)
        sV.wait(g)
        for i in range(8):
            sWA.dma(g.dma_start(out=acat_sb[:, 512 * i:512 * (i + 1)], in_=ap["acat"][i]))
        for i in range(8):
            sWA.dma(g.dma_start(out=bcat_sb[:, 512 * i:512 * (i + 1)],
                               in_=ap["bcat"][:, 512 * i:512 * (i + 1)]))
    W_AB = sWA.n

    @blk.tensor
    def _(t):
        t.wait_ge(sWA.h, W_AB)
        sE.wait(t)
        last = None
        for i in range(32):
            lh = acat_sb[:, 512 * (i // 4) + 128 * (i % 4):512 * (i // 4) + 128 * (i % 4 + 1)]
            for tb in range(2):
                last = t.matmul(bank(tb), lh,
                                bigA[:, TPC * i + 512 * tb:TPC * i + 512 * (tb + 1)],
                                start=(i == 0), stop=(i == 31))
        sT.inc(last)

    @blk.scalar
    def _(sc):
        sT.wait(sc)
        sS.inc(sc.activation(tmp_sb[:, 0:512], bank(0), AF.Identity, bias=0.0, scale=1.0))
        sS.inc(sc.activation(tmp_sb[:, 512:1024], bank(1), AF.Identity, bias=0.0, scale=1.0))

    @blk.gpsimd
    def _(g):
        g.wait_ge(sC.h, C_CNT)
        sE.wait(g)
        sWC.dma(g.dma_start(out=alw[:], in_=cnt_out.ap()))
    W_CNTRD = sWC.n

    @blk.vector
    def _(v):
        v.wait_ge(sWC.h, W_CNTRD)
        sV.inc(v.tensor_scalar(out=alw[:], in0=alw[:], scalar1=CAPACITY, scalar2=None,
                               op0=OP.is_le))

    @blk.tensor
    def _(t):
        sV.wait(t)
        t.matmul(P[1][:, 0:E], C["onesrow"][:], alw[0:1, 0:E], start=True, stop=True)
        sT.inc(t.matmul(P[1][:, E:2 * E], C["onesrow"][:], alw[0:1, E:2 * E],
                        start=True, stop=True))

    @blk.vector
    def _(v):
        sT.wait(v)
        sV.inc(v.tensor_copy(abc[:, 0:2 * E], P[1][:, 0:2 * E]))

    for tc in range(8):
        @blk.vector
        def _(v, tc=tc):
            v.wait_ge(sV.h, sV.n)
            sV.inc(v.tensor_tensor(scrA[:, 0:E], ucol[:, 16 * tc:16 * tc + E],
                                   abc[:, 0:E], OP.mult))
            sV.inc(v.tensor_tensor(scrA[:, E:2 * E], ucol[:, 16 * tc + E:16 * tc + 2 * E],
                                   abc[:, E:2 * E], OP.mult))
            v.wait_ge(sV.h, sV.n)
            sV.inc(v.tensor_tensor(wg8[:, E * tc:E * (tc + 1)], scrA[:, 0:E],
                                   scrA[:, E:2 * E], OP.add))

        @blk.tensor
        def _(t, tc=tc):
            sV.wait(t)
            sT.inc(t.transpose(P[1][0:E, 128:256], wg8[:, E * tc:E * (tc + 1)],
                               C["identf"][:]))

        @blk.vector
        def _(v, tc=tc):
            sT.wait(v)
            sV.inc(v.tensor_copy(wgT[:, 128 * tc:128 * (tc + 1)], P[1][0:E, 128:256]))

    @blk.tensor
    def _(t):
        sV.wait(t)
        t.matmul(P[1][:, 0:512], C["e16"][:], wgT[:, 0:512], start=True, stop=True)
        sT.inc(t.matmul(P[1][:, 512:1024], C["e16"][:], wgT[:, 512:1024],
                        start=True, stop=True))

    @blk.vector
    def _(v):
        sT.wait(v)
        sS.wait(v)
        sV.inc(v.tensor_tensor(tmps_sb[:, 0:512], tmp_sb[:, 0:512], P[1][:, 0:512], OP.mult))
        sV.inc(v.tensor_tensor(tmps_sb[:, 512:1024], tmp_sb[:, 512:1024],
                               P[1][:, 512:1024], OP.mult))
    V_TMPS = sV.n

    def ffn_inject(og, m, tb):
        oc = og * 2 + m
        return (bcat_sb[:, 128 * oc:128 * (oc + 1)],
                tmps_sb[:, 512 * tb:512 * (tb + 1)])

    def ffn_evac(og, m, tb, ps, u_mark, t_mark):
        oc = og * 2 + m

        @blk.scalar
        def _(sc, oc=oc, tb=tb, ps=ps, t_mark=t_mark):
            sc.wait_ge(sT.h, t_mark)
            sE.inc(sc.activation(bigB[:, TPC * oc + 512 * tb:TPC * oc + 512 * (tb + 1)],
                                 ps, AF.Silu, bias=0.0, scale=1.0))

    def gate_ffn(t):
        t.wait_ge(sV.h, V_TMPS)

    emit_proj(ap["wt"], 16, ffn_evac, inject_fn=ffn_inject, first_gate=gate_ffn,
              bank_fn=bank8)
    E_FFN = sE.n
    U_FFN = sU.n
    phf.close()
    if stop_phase == 4:
        srcs = [(bigB[:, TPC * i:TPC * (i + 1)], 128, TPC) for i in range(6)]
        srcs.append((tmps_sb[:, 0:1024], 128, 1024))
        srcs.append((wgT[0:8, 0:1024], 8, 1024))
        ph4.close()
        return emit_stop_dump(srcs, stacks=(phf, phE))
    ph4.close()
    phE.close()

    ph5 = contextlib.ExitStack()
    kta = ph5.enter_context(nc.sbuf_tensor("kta", [128, 2 * T], BF16))
    vtm = ph5.enter_context(nc.sbuf_tensor("vtm", [128, 2 * T], BF16))
    pT = ph5.enter_context(nc.sbuf_tensor("pT", [128, 3 * 1024], BF16))
    psacc = ph5.enter_context(nc.sbuf_tensor("psacc", [128, 512], F32))
    pbf = ph5.enter_context(nc.sbuf_tensor("pbf", [128, 512], BF16))
    usb = ph5.enter_context(nc.sbuf_tensor("usb", [128, 512], BF16))
    rz_sb = ph5.enter_context(nc.sbuf_tensor("rz_sb", [1, 512], F32))
    rzbc_sb = ph5.enter_context(nc.sbuf_tensor("rzbc_sb", [128, 512], F32))
    obuf = ph5.enter_context(nc.sbuf_tensor("obuf", [128, 2 * 512], BF16))
    bbf = ph5.enter_context(nc.sbuf_tensor("bbf", [1, 1], BF16))
    ph5a = contextlib.ExitStack()
    vta = ph5a.enter_context(nc.sbuf_tensor("vta", [128, 2 * T], BF16))
    q2b = vta
    phb5a = ph5a.enter_context(nc.psum_tensor("phb5a", [128, 512], BF16))
    phb5b = ph5a.enter_context(nc.psum_tensor("phb5b", [128, 512], BF16))

    @blk.gpsimd
    def _(g):
        g.wait_ge(sC.h, C_QKV)
        g.wait_ge(sU.h, U_FFN)
        g.wait_ge(sV.h, sV.n)
        sS.wait(g)
        sT.wait(g)
        rankv = nc.values_load(rkt[:], engines=[mybir.EngineType.Pool],
                               min_val=0, max_val=7, skip_runtime_bounds_check=True)
        qo = qkv_out.ap()
        for c in g.Switch(rankv, 8):
            hb0 = 1024 * (c % 4)
            kb0 = 256 * (c % 4)
            for hc in range(8):
                for r in range(4):
                    g.dma_start(out=bigA[:, 4096 * hc + 1024 * r:4096 * hc + 1024 * (r + 1)],
                                in_=qo[6144 * r + hb0 + 128 * hc:6144 * r + hb0 + 128 * (hc + 1), :]
                                ).then_inc(sG.h, 16)
            for k2 in range(2):
                for r in range(4):
                    g.dma_start(out=kta[:, 4096 * k2 + 1024 * r:4096 * k2 + 1024 * (r + 1)],
                                in_=qo[6144 * r + 4096 + kb0 + 128 * k2:
                                       6144 * r + 4096 + kb0 + 128 * (k2 + 1), :]
                                ).then_inc(sG.h, 16)
            for k2 in range(2):
                for r in range(4):
                    g.dma_start(out=vta[:, 4096 * k2 + 1024 * r:4096 * k2 + 1024 * (r + 1)],
                                in_=qo[6144 * r + 5120 + kb0 + 128 * k2:
                                       6144 * r + 5120 + kb0 + 128 * (k2 + 1), :]
                                ).then_inc(sG.h, 16)
    sG.n += 48 * 16
    G_ASM = sG.n

    VT_BASE = [0]
    VT_BASE[0] = sV.n
    for kc in range(32):
        reg = phb5a if kc % 2 == 0 else phb5b

        @blk.tensor
        def _(t, kc=kc, reg=reg):
            if kc == 0:
                t.wait_ge(sG.h, G_ASM)
                sE.wait(t)
                sS.wait(t)
            elif kc >= 2:
                v_need = VT_BASE[0] + kc - 1
                t.wait_ge(sV.h, v_need)
            last = None
            for k2 in range(2):
                last = t.transpose(reg[:, 128 * k2:128 * (k2 + 1)],
                                   vta[:, 4096 * k2 + 128 * kc:4096 * k2 + 128 * (kc + 1)],
                                   C["ident"][:])
            sT.inc(last)

        @blk.vector
        def _(v, kc=kc, reg=reg):
            sT.wait(v)
            sV.inc(v.tensor_copy(vtm[:, 256 * kc:256 * (kc + 1)], reg[:, 0:256]))


    @blk.vector
    def _(v):
        sV.inc(v.memset(st[:, 24:26], 0.0))
    for grp in range(10):
        src = (bigA if grp < 8 else kta)
        col = 4096 * grp if grp < 8 else 4096 * (grp - 8)

        @blk.vector
        def _(v, src=src, col=col, grp=grp):
            if grp == 0:
                v.wait_ge(sG.h, G_ASM)
            v.wait_ge(sV.h, sV.n)
            sV.inc(v.tensor_tensor(q2b[:, 0:4096], src[:, col:col + 4096],
                                   src[:, col:col + 4096], OP.mult))

        for half in range(2):
            @blk.tensor
            def _(t, grp=grp, half=half):
                sV.wait(t)
                last = None
                for s4 in range(4):
                    s8 = 4 * half + s4
                    last = t.matmul(bank(s4)[0:1, :],
                                    C["onescol"][:], q2b[:, 512 * s8:512 * (s8 + 1)],
                                    start=True, stop=True)
                sT.inc(last)

            @blk.vector
            def _(v, grp=grp, half=half):
                sT.wait(v)
                dst = st[:, 24:25] if grp < 8 else st[:, 25:26]
                for s4 in range(4):
                    v.wait_ge(sV.h, sV.n)
                    sV.inc(v.tensor_reduce(st[0:1, 26:27], bank(s4)[0:1, :], AX.X, OP.max))
                    v.wait_ge(sV.h, sV.n)
                    sV.inc(v.tensor_tensor(dst[0:1, :], dst[0:1, :], st[0:1, 26:27], OP.max))

    @blk.scalar
    def _(sc):
        sV.wait(sc)
        sS.inc(sc.activation(st[0:1, 27:28], st[0:1, 24:25], (AF.Identity if NOLN else AF.Ln), bias=LN_EPS, scale=1.0))
        sS.inc(sc.activation(st[0:1, 28:29], st[0:1, 25:26], (AF.Identity if NOLN else AF.Ln), bias=LN_EPS, scale=1.0))

    @blk.vector
    def _(v):
        sS.wait(v)
        sV.inc(v.tensor_tensor(st[0:1, 27:28], st[0:1, 27:28], st[0:1, 28:29], OP.add))

    @blk.scalar
    def _(sc):
        sV.wait(sc)
        sS.inc(sc.activation(st[0:1, 29:30], st[0:1, 27:28], AF.Exp, bias=0.0, scale=0.5))

    @blk.vector
    def _(v):
        sS.wait(v)
        sV.inc(v.tensor_copy(bbf[:], st[0:1, 29:30]))

    @blk.tensor
    def _(t):
        sV.wait(t)
        sT.inc(t.matmul(P[2][:, 0:1], C["negrow"][:], bbf[:], start=True, stop=True))

    @blk.vector
    def _(v):
        sT.wait(v)
        sV.inc(v.tensor_copy(negb[:], P[2][:, 0:1]))
    V_NEGB = sV.n
    ph5a.close()
    ph5b = contextlib.ExitStack()
    pzr = ph5b.enter_context(nc.psum_tensor("pzr", [128, 1024], F32))

    sTZ = Sem(ctx, nc, "sTZ")
    sTR = Sem(ctx, nc, "sTR")
    sZ1 = Sem(ctx, nc, "sZ1")
    sZ2 = Sem(ctx, nc, "sZ2")
    sZ3 = Sem(ctx, nc, "sZ3")

    blocks = []
    gp0 = 0
    for hb in range(8):
        for qb in range(8):
            blocks.append((hb, qb, 2 * qb + 2, gp0))
            gp0 += 2 * qb + 2
    NPAIR_TOT = gp0

    @blk.tensor
    def _(t):
        for Bi, (hb, qb, npair, g0) in enumerate(blocks):
            kvl = hb // 4
            obank = bank(4 + Bi % 2)
            for bp in range(npair):
                gp = g0 + bp
                sp = P[gp % 2]
                if Bi == 0 and bp == 0:
                    t.wait_ge(sV.h, V_NEGB)
                if gp >= 2:
                    t.wait_ge(sEX.h, gp - 1)
                last = None
                for j in range(2):
                    kc = 2 * bp + j
                    diag = kc >= 4 * qb
                    last = t.matmul(
                        sp[:, 512 * j:512 * (j + 1)],
                        kta[:, 4096 * kvl + 128 * kc:4096 * kvl + 128 * (kc + 1)],
                        bigA[:, 4096 * hb + 512 * qb:4096 * hb + 512 * (qb + 1)],
                        start=True, stop=not diag)
                    if diag:
                        d = kc - 4 * qb
                        last = t.matmul(sp[:, 512 * j:512 * (j + 1)], C["rampa"][:],
                                        C["rampb"][:, 512 * d:512 * (d + 1)],
                                        start=False, stop=True)
                sSC.inc(last)
                if Bi >= 1 and bp == 1:
                    t.wait_ge(sZ1.h, Bi)
                    if Bi >= 2:
                        t.wait_ge(sZ2.h, 2 * Bi - 3)
                    sTZ.inc(t.matmul(pzr[0:1, 0:512], C["onescol"][:], pbf[:],
                                     start=True, stop=True))
                if Bi >= 1 and bp == min(2, npair - 1):
                    t.wait_ge(sZ2.h, 2 * Bi)
                    if Bi >= 2:
                        t.wait_ge(sZ3.h, Bi - 1)
                    sTR.inc(t.matmul(pzr[:, 512:1024], C["onesrow"][:], rz_sb[:],
                                     start=True, stop=True))
                if bp >= 1:
                    gpa = gp - 1
                    bpa = bp - 1
                    t.wait_ge(sEX.h, gpa + 1)
                    if bpa == 0 and Bi >= 2:
                        t.wait_ge(sOE.h, Bi - 1)
                    lastav = None
                    for j in range(2):
                        kc = 2 * bpa + j
                        lastav = t.matmul(
                            obank,
                            vtm[:, 256 * kc + 128 * kvl:256 * kc + 128 * (kvl + 1)],
                            pT[:, 1024 * (gpa % 3) + 512 * j:
                                1024 * (gpa % 3) + 512 * (j + 1)],
                            start=(bpa == 0 and j == 0), stop=False)
                    sAV.inc(lastav)
            gpa = g0 + npair - 1
            t.wait_ge(sEX.h, gpa + 1)
            if npair - 1 == 0 and Bi >= 2:
                t.wait_ge(sOE.h, Bi - 1)
            lastav = None
            for j in range(2):
                kc = 2 * (npair - 1) + j
                lastav = t.matmul(
                    obank,
                    vtm[:, 256 * kc + 128 * kvl:256 * kc + 128 * (kvl + 1)],
                    pT[:, 1024 * (gpa % 3) + 512 * j:1024 * (gpa % 3) + 512 * (j + 1)],
                    start=(npair == 1 and j == 0), stop=(j == 1))
            sAV.inc(lastav)
        Bl = len(blocks)
        t.wait_ge(sZ1.h, Bl)
        t.wait_ge(sZ2.h, 2 * Bl - 3)
        sTZ.inc(t.matmul(pzr[0:1, 0:512], C["onescol"][:], pbf[:], start=True, stop=True))
        t.wait_ge(sZ2.h, 2 * Bl)
        t.wait_ge(sZ3.h, Bl - 1)
        sTR.inc(t.matmul(pzr[:, 512:1024], C["onesrow"][:], rz_sb[:], start=True, stop=True))

    @blk.scalar
    def _(sc):
        for Bi, (hb, qb, npair, g0) in enumerate(blocks):
            for bp in range(npair):
                gp = g0 + bp
                sc.wait_ge(sSC.h, gp + 1)
                if gp >= 3:
                    sc.wait_ge(sAV.h, gp - 2)
                    sc.wait_ge(sUA.h, gp - 2)
                sEX.inc(sc.activation(pT[:, 1024 * (gp % 3):1024 * (gp % 3 + 1)],
                                      P[gp % 2][:], AF.Exp, bias=negb[:, 0:1], scale=1.0))
                if Bi >= 1 and bp == 1:
                    sc.wait_ge(sTZ.h, Bi)
                    sZ2.inc(sc.activation(rz_sb[:], pzr[0:1, 0:512], (AF.Identity if NOLN else AF.Ln),
                                          bias=0.0, scale=1.0))
                    sc.wait_ge(sZ2.h, 2 * (Bi - 1) + 1)
                    sZ2.inc(sc.activation(rz_sb[:], rz_sb[:], AF.Exp, bias=0.0, scale=-1.0))
        sc.wait_ge(sTZ.h, len(blocks))
        sZ2.inc(sc.activation(rz_sb[:], pzr[0:1, 0:512], (AF.Identity if NOLN else AF.Ln),
                              bias=0.0, scale=1.0))
        sc.wait_ge(sZ2.h, 2 * len(blocks) - 1)
        sZ2.inc(sc.activation(rz_sb[:], rz_sb[:], AF.Exp, bias=0.0, scale=-1.0))

    @blk.vector
    def _(v):
        vq = []
        for Bi, (hb, qb, npair, g0) in enumerate(blocks):
            for bp in range(npair):
                gp = g0 + bp
                v.wait_ge(sEX.h, gp + 1)
                sV.inc(v.tensor_tensor(usb[:], pT[:, 1024 * (gp % 3):1024 * (gp % 3) + 512],
                                       pT[:, 1024 * (gp % 3) + 512:1024 * (gp % 3 + 1)],
                                       OP.add))
                v.wait_ge(sV.h, sV.n)
                if bp == 0:
                    sUA.inc(v.tensor_copy(psacc[:], usb[:]))
                else:
                    sUA.inc(v.tensor_tensor(psacc[:], psacc[:], usb[:], OP.add))
                if vq:
                    vq.pop(0)()
            while vq:
                vq.pop(0)()
            gend = g0 + npair

            def mk(Bi=Bi, hb=hb, qb=qb, gend=gend):
                obank = bank(4 + Bi % 2)

                def s_pbf():
                    v.wait_ge(sUA.h, gend)
                    sZ1.inc(v.tensor_copy(pbf[:], psacc[:]))

                def s_rzc():
                    v.wait_ge(sTR.h, Bi + 1)
                    sZ3.inc(v.tensor_copy(rzbc_sb[:], pzr[:, 512:1024]))

                def s_oev():
                    v.wait_ge(sZ3.h, Bi + 1)
                    v.wait_ge(sAV.h, gend)
                    if Bi >= 2:
                        v.wait_ge(sPO[Bi % 2].h, 16 * (Bi // 2))
                    sOE.inc(v.tensor_tensor(obuf[:, 512 * (Bi % 2):512 * (Bi % 2 + 1)],
                                            obank, rzbc_sb[:], OP.mult))
                return [s_pbf, s_rzc, s_oev]
            steps = mk()
            steps[0]()
            vq = steps[1:]
        while vq:
            vq.pop(0)()

    @blk.sync
    def _(s):
        for Bi, (hb, qb, npair, g0) in enumerate(blocks):
            s.wait_ge(sOE.h, Bi + 1)
            sPO[Bi % 2].dma(s.dma_start(
                out=o_in.ap()[128 * hb:128 * (hb + 1), 512 * qb:512 * (qb + 1)],
                in_=obuf[:, 512 * (Bi % 2):512 * (Bi % 2 + 1)]))

    @blk.gpsimd
    def _(g):
        g.wait_ge(sPO[0].h, 16 * 16)
        g.wait_ge(sPO[1].h, 16 * 16)
        sC.inc(g.collective_compute("AllGather", OP.bypass, replica_groups=GROUPS_HALF,
                                    ins=[o_in.ap()[0:512, :]], outs=[o_out_a.ap()]))
        g.wait_ge(sPO[0].h, sPO[0].n)
        g.wait_ge(sPO[1].h, sPO[1].n)
        sC.inc(g.collective_compute("AllGather", OP.bypass, replica_groups=GROUPS_HALF,
                                    ins=[o_in.ap()[512:1024, :]], outs=[o_out_b.ap()]))
    C_O = sC.n
    ph5b.close()
    if stop_phase == 5:
        @blk.gpsimd
        def _(g):
            g.wait_ge(sC.h, C_O)
        srcs = [(o_in.ap()[128 * i:128 * (i + 1), 0:4096], 128, 4096) for i in range(4)]
        ph5.close()
        return emit_stop_dump(srcs)
    ph5.close()

    ph6 = contextlib.ExitStack()
    phb6 = ph6.enter_context(nc.psum_tensor("phb6", [128, 512], BF16))
    rth = ph6.enter_context(nc.sbuf_tensor("rth", [128, 4 * TPC], BF16))
    xr = ph6.enter_context(nc.sbuf_tensor("xr", [128, 2 * 4096], F32))
    outb = ph6.enter_context(nc.sbuf_tensor("outb", [128, 2 * 512], F32))

    N_PAIRS = 8 * sum(2 * qb + 2 for qb in range(8))
    PERM_A = [8 * r + h for r in range(4) for h in range(4)]
    PERM_B = [8 * r + 4 + h for r in range(4) for h in range(4)]
    WO_PERM = PERM_A + PERM_B

    @blk.gpsimd
    def _(g):
        g.wait_ge(sC.h, C_O - 1)
        g.wait_ge(sSC.h, N_PAIRS)
        g.wait_ge(sAV.h, N_PAIRS)
        rankv2 = nc.values_load(rkt[:], engines=[mybir.EngineType.Pool],
                                min_val=0, max_val=7, skip_runtime_bounds_check=True)
        ooa = o_out_a.ap()
        for c in g.Switch(rankv2, 8):
            t0 = 1024 * (c % 4)
            for r in range(4):
                for h in range(4):
                    fc = 8 * r + h
                    g.dma_start(out=bigA[:, TPC * fc:TPC * (fc + 1)],
                                in_=ooa[512 * r + 128 * h:512 * r + 128 * (h + 1),
                                        t0:t0 + TPC]).then_inc(sG.h, 16)
    sG.n += 16 * 16
    G_OASS_A = sG.n

    @blk.gpsimd
    def _(g):
        g.wait_ge(sC.h, C_O)
        rankv3 = nc.values_load(rkt[:], engines=[mybir.EngineType.Pool],
                                min_val=0, max_val=7, skip_runtime_bounds_check=True)
        oob = o_out_b.ap()
        for c in g.Switch(rankv3, 8):
            t0 = 1024 * (c % 4)
            for r in range(4):
                for h in range(4):
                    fc = 8 * r + 4 + h
                    g.dma_start(out=bigA[:, TPC * fc:TPC * (fc + 1)],
                                in_=oob[512 * r + 128 * h:512 * r + 128 * (h + 1),
                                        t0:t0 + TPC]).then_inc(sG.h, 16)
    sG.n += 16 * 16
    G_OASS = sG.n

    def mk_wo_evac(ocg):
        def wo_evac(og, m, tb, ps, u_mark, t_mark):
            ocl = og * 2 + m
            oc = 4 * ocg + ocl

            @blk.vector
            def _(v, ocl=ocl, oc=oc, tb=tb, ps=ps, u_mark=u_mark):
                v.wait_ge(sU.h, u_mark)
                sE.inc(v.tensor_tensor(
                    rth[:, TPC * ocl + 512 * tb:TPC * ocl + 512 * (tb + 1)],
                    ps, bigB[:, TPC * oc + 512 * tb:TPC * oc + 512 * (tb + 1)], OP.mult))
        return wo_evac

    def mk_wo_gate(ocg):
        def gate(t):
            if ocg == 0:
                sOE.wait(t)
            sE.wait(t)
            sV.wait(t)
        return gate

    for ocg in range(8):
        @blk.sync
        def _(s, ocg=ocg):
            if ocg == 0:
                s.wait_ge(sSC.h, sSC.n)
                s.wait_ge(sAV.h, sAV.n)
                s.wait_ge(sUA.h, sUA.n)
                s.wait_ge(sOE.h, sOE.n)
                s.wait_ge(sV.h, sV.n)
            if ocg >= 2:
                s.wait_ge(sV.h, sV.n)
            for tc in range(8):
                sXs[0].dma(s.dma_start(
                    out=xr[:, 4096 * (ocg % 2) + 512 * tc:4096 * (ocg % 2) + 512 * (tc + 1)],
                    in_=ap["x"][128 * tc:128 * (tc + 1), 512 * ocg:512 * (ocg + 1)]))
        X_OCG = sXs[0].n

        def wo_igate(t, i, ocg=ocg):
            if i < 16:
                t.wait_ge(sG.h, G_OASS_A)
            else:
                t.wait_ge(sG.h, G_OASS)
        emit_proj(ap["wo"][2 * ocg:2 * ocg + 2], 2, mk_wo_evac(ocg),
                  first_gate=mk_wo_gate(ocg), serial4=True, iperm=WO_PERM,
                  i_gate=wo_igate)

        for tc in range(8):
            reg = phb6

            @blk.tensor
            def _(t, ocg=ocg, tc=tc, reg=reg):
                if tc == 0:
                    sE.wait(t)
                else:
                    sV.wait(t)
                last = None
                for j in range(4):
                    last = t.transpose(
                        reg[:, 128 * j:128 * (j + 1)],
                        rth[:, TPC * j + 128 * tc:TPC * j + 128 * (tc + 1)],
                        C["ident"][:])
                sT.inc(last)

            @blk.vector
            def _(v, ocg=ocg, tc=tc, reg=reg, X_OCG=X_OCG):
                sT.wait(v)
                v.wait_ge(sXs[0].h, X_OCG)
                gidx = 8 * ocg + tc
                if gidx >= 2:
                    v.wait_ge(sOs[gidx % 2].h, 16 * (gidx // 2))
                sV.inc(v.tensor_tensor(
                    outb[:, 512 * (tc % 2):512 * (tc % 2 + 1)], reg[:, 0:512],
                    xr[:, 4096 * (ocg % 2) + 512 * tc:4096 * (ocg % 2) + 512 * (tc + 1)],
                    OP.add))

            @blk.sync
            def _(s, ocg=ocg, tc=tc):
                sV.wait(s)
                gidx = 8 * ocg + tc
                sOs[gidx % 2].dma(s.dma_start(
                    out=y_out[128 * tc:128 * (tc + 1), 512 * ocg:512 * (ocg + 1)],
                    in_=outb[:, 512 * (tc % 2):512 * (tc % 2 + 1)]))

    @blk.sync
    def _(s):
        s.wait_ge(sOs[0].h, sOs[0].n)
        s.wait_ge(sOs[1].h, sOs[1].n)

    @blk.gpsimd
    def _(g):
        g.wait_ge(sOs[0].h, sOs[0].n)
        g.wait_ge(sOs[1].h, sOs[1].n)

    ph6.close()
    ctx.close()
    return nc



_CACHE = {}


def kernel(**inputs):
    in_maps = _prep_inputs(inputs)
    if "nc" not in _CACHE:
        _CACHE["nc"] = build_program(debug=False)
    nc = _CACHE["nc"]
    res = run_bass_kernel_spmd(nc, in_maps, core_ids=list(range(NCORE)))
    y = np.concatenate([res.results[c]["y"] for c in range(NCORE)], axis=0)
    return y.reshape(B, T, D).astype(np.float32)
